# version 11
# baseline (speedup 1.0000x reference)
import numpy as np
from contextlib import ExitStack
import concourse.bass as bass
import concourse.mybir as mybir
from concourse.bass_utils import run_bass_kernel_spmd

F32 = mybir.dt.float32
F32R = mybir.dt.float32r
BF16 = mybir.dt.bfloat16
I32 = mybir.dt.int32
ALU = mybir.AluOpType
ACT = mybir.ActivationFunctionType
AX = mybir.AxisListType

B, T, DM = 4, 4096, 1024
DH, NST, DTR = 1024, 16, 64
TBL = 512
NB = T // TBL
NTC = TBL // 128
NDC = DH // 128
TOWN = T // 2
NOC = TOWN // 128
INNER = 4096
NES = INNER // 512
NTB = TOWN // 128
NTS = 4
RG = [[0, 1], [2, 3], [4, 5], [6, 7]]

DEBUG = False


class Prog:
    ENGINES = ("sync", "gpsimd", "vector", "scalar", "tensor")

    def __init__(self):
        self.ops = {e: [] for e in self.ENGINES}
        self.cnt = {}
        self.guard = {}

    def c(self, sem):
        return self.cnt.get(sem, 0)

    def add(self, eng, builder, waits=(), inc=None):
        waits = tuple((s, v) for (s, v) in waits if v > 0)
        if inc is not None:
            self.cnt[inc[0]] = self.c(inc[0]) + inc[1]
        self.ops[eng].append((builder, waits, inc))

    def dma(self, eng, out, in_, waits=(), sem=None, slow=False):
        semn = sem or ("gs" if eng == "gpsimd" else "ws")
        self.add(eng, lambda h, o=out, i=in_: h.dma_start(
            out=o, in_=i, allow_slow_non_contiguous=slow),
            waits=waits, inc=(semn, 16))
        return (semn, self.c(semn))

    def setg(self, name, sem):
        self.guard[name] = (sem, self.c(sem))

    def g(self, name):
        return self.guard.get(name, ("vv", 0))

    def emit(self, block, sems):
        for eng in self.ENGINES:
            ops = list(self.ops[eng])

            def mk(ops):
                def run(h):
                    for builder, waits, inc in ops:
                        for s, v in waits:
                            h.wait_ge(sems[s], v)
                        inst = builder(h)
                        if inc is not None:
                            inst.then_inc(sems[inc[0]], inc[1])
                return run
            getattr(block, eng)(mk(ops))


def build_program(debug=DEBUG):
    nc = bass.Bass("TRN2", target_bir_lowering=False)
    p = Prog()
    dr = {}

    def din(name, shape, dtype=F32):
        dr[name] = nc.dram_tensor(name, list(shape), dtype,
                                  kind="ExternalInput").ap()
        return dr[name]

    def dout(name, shape, dtype=F32):
        dr[name] = nc.dram_tensor(name, list(shape), dtype,
                                  kind="ExternalOutput").ap()
        return dr[name]

    _CC = {"ar_in_d", "ar_out_d", "m4p_d", "rs_out_d"}

    def dint(name, shape, dtype=F32):
        kind = "ExternalOutput" if (debug and name not in _CC) else "Internal"
        dr[name] = nc.dram_tensor(name, list(shape), dtype, kind=kind).ap()
        return dr[name]

    x_full = din("x_full", [T, DM])
    x_own = din("x_own", [TOWN, DM])
    w_in_x = din("w_in_x", [DH, DM])
    w_in_z = din("w_in_z", [DH, DM])
    conv_w = din("conv_w", [DH, 4])
    conv_b = din("conv_b", [DH, 1])
    w_x = din("w_x", [96, DH])
    w_dt = din("w_dt", [DH, DTR])
    b_dt = din("b_dt", [DH, 1])
    w_out = din("w_out", [DM, DH])
    a_log = din("a_log", [DH, NST])
    d_vec = din("d_vec", [DH, 1])
    w_gate = din("w_gate", [INNER, DM])
    w_up = din("w_up", [INNER, DM])
    w_down = din("w_down", [DM, INNER])
    ident = din("ident", [128, 128])
    out = dout("out", [TOWN, DM])

    w_in_xT_d = dint("w_in_xT_d", [NDC, DM, 128], F32R)
    w_in_zT_d = dint("w_in_zT_d", [NDC, DM, 128], F32R)
    w_outT_d = dint("w_outT_d", [DM, DM], F32R)
    gT_g_d = dint("gT_g_d", [32, DM, 128], BF16)
    gT_u_d = dint("gT_u_d", [32, DM, 128], BF16)
    gdT_d = dint("gdT_d", [NDC, INNER, 128], BF16)
    kx_d = dint("kx_d", [DM, TOWN], BF16)
    ar_in_d = dint("ar_in_d", [NB, 96, TBL])
    ar_out_d = dint("ar_out_d", [NB, 96, TBL])
    m4p_d = dint("m4p_d", [T, DM])
    rs_out_d = dint("rs_out_d", [TOWN, DM])
    gu_d = dint("gu_d", [TOWN, INNER])
    kgu_d = dint("kgu_d", [INNER, TOWN], BF16)
    ffn_d = dint("ffn_d", [TOWN, DM])
    beta_d = dint("beta_d", [1, TOWN])
    mwrow_d = dint("mwrow_d", [1, 4])
    if debug:
        dint("dbg_xntm", [T, DM], F32R)
        dint("dbg_ys", [DH, T])
        dint("dbg_m4post", [8, 128, DM])
        dint("dbg_xraw", [T, DM])
        dint("dbg_xdbl", [NB, 96, TBL])
        dint("dbg_m4p", [T, DM])
        dint("dbg_rs", [TOWN, DM])
        dint("dbg_u", [DH, T], F32R)
        dint("dbg_dt", [DH, T])
        dint("dbg_y", [DH, T], F32R)
        dint("dbg_xn", [DM, T], F32R)

    AR_W = 10240

    with ExitStack() as _st:
        def sb(name, shape, dtype):
            return _st.enter_context(nc.sbuf_tensor(name, shape, dtype))

        def pt(name, shape, dtype):
            return _st.enter_context(nc.psum_tensor(name, shape, dtype))

        def sem(name):
            return _st.enter_context(nc.semaphore(name))

        w_xT = sb("w_xT", [128, NDC, 96], F32R)
        w_dtT = sb("w_dtT", [64, NDC, 128], F32R)
        negA = sb("negA", [128, NDC, NST], F32)
        convw = sb("convw", [128, NDC, 4], F32)
        convb = sb("convb", [128, NDC, 1], F32)
        bdt = sb("bdt", [128, NDC, 1], F32)
        dvec = sb("dvec", [128, NDC, 1], F32)
        hst = sb("hst", [128, NDC, NST], F32)
        rsx = sb("rsx", [128, NOC], F32)
        mxgu = sb("mxgu", [128, NTB], F32)
        idf = sb("idf", [128, 128], F32)
        idr = sb("idr", [128, 128], F32R)
        idb = sb("idb", [128, 128], BF16)
        mw = sb("mw", [128, 4], F32)
        colscr = sb("colscr", [128, 19], F32)
        u_sb = sb("u_sb", [128, NDC, TBL], F32R)
        dtc = sb("dtc", [128, NDC, TBL], F32)
        dtu = sb("dtu", [128, NDC, TBL], F32)
        xn_fm = sb("xn_fm", [128, NDC, TBL], F32R)
        xn_tm = sb("xn_tm", [128, DM], F32R)
        xc = sb("xc", [128, 2, TBL + 3], F32)
        xtail = sb("xtail", [128, NDC, 3], F32)
        cstage = sb("cstage", [128, NDC, TBL], F32)
        cbuf = sb("cbuf", [128, 2, TBL], F32)
        wcol = sb("wcol", [128, 2, NDC, 128], F32R)
        kxt = sb("kxt", [128, 2, NDC, 128], BF16)
        dA_sb = sb("dA", [128, 2, TBL], F32)
        dBu_sb = sb("dBu", [128, 2, TBL], F32)
        hseq_sb = sb("hseq", [128, 2, TBL], F32)
        brep = sb("brep", [128, 2, TBL], F32)
        crep = sb("crep", [128, 2, TBL], F32)
        xdbl = sb("xdbl", [96, TBL], F32)
        xdstg = sb("xdstg", [96, TBL], F32)
        dtraw = sb("dtraw", [64, TBL], F32R)
        stg = sb("stg", [128, NDC, 128], F32)
        arena = sb("arena", [128, AR_W], F32)
        sq0 = sb("sq0", [128, 2], F32)
        yacc_sb = sb("yacc_sb", [128, NDC, TBL], F32)
        ps0 = pt("ps0", [128, 512], F32)
        ps1 = pt("ps1", [128, 512], F32)
        ps2 = pt("ps2", [128, 512], F32)
        ps3 = pt("ps3", [128, 512], F32)
        psTb = pt("psTb", [128, 1024], BF16)
        psTr = pt("psTr", [128, 1024], F32R)
        block = _st.enter_context(nc.Block())

        PS = [ps0, ps1, ps2, ps3]
        ps_state = [("vv", 0)] * 4
        ps_rr = [0]

        def ps_acquire():
            i = ps_rr[0] % 4
            ps_rr[0] += 1
            return PS[i], ps_state[i], i

        def ps_release(i, sem):
            ps_state[i] = (sem, p.c(sem))

        def av(off, shape, dtype=F32):
            n = 1
            for s in shape[1:]:
                n *= s
            words = n * mybir.dt.size(dtype) // 4
            assert off + words <= AR_W, (off, words)
            v = arena[:, off:off + words]
            if dtype != F32:
                v = v.bitcast(dtype)
            if len(shape) == 3:
                v = v.rearrange("p (a b) -> p a b", a=shape[1])
            return v

        lnscr = av(6144, [128, DM])
        wsum = colscr[:, 13:17]
        zscr = cbuf
        wtbuf = wcol[:].rearrange("p a b c -> p a (b c)")
        m4stg = av(5120, [128, DM])
        p.dma("sync", idf[:], ident)
        p.dma("gpsimd", idr[:], ident)
        p.dma("gpsimd", idb[:], ident)
        r3 = lambda ap: ap.rearrange("(dc q) n -> q dc n", q=128)
        p.dma("sync", convw[:], r3(conv_w))
        p.dma("sync", convb[:], r3(conv_b), slow=True)
        p.dma("sync", bdt[:], r3(b_dt), slow=True)
        p.dma("sync", dvec[:], r3(d_vec), slow=True)
        p.dma("sync", negA[:], r3(a_log))
        ldc = p.c("ws")
        ldg = p.c("gs")
        p.add("scalar", lambda h: h.activation(negA[:], negA[:], ACT.Exp),
              waits=[("ws", ldc)], inc=("ss", 1))
        p.add("vector", lambda h: h.tensor_scalar(
            negA[:], negA[:], -1.0, None, ALU.mult),
            waits=[("ss", p.c("ss"))], inc=("vv", 1))
        p.add("vector", lambda h: h.memset(hst[:], 0.0), inc=("vv", 1))
        p.add("vector", lambda h: h.memset(xtail[:], 0.0), inc=("vv", 1))

        def pe_T(src, nsub, psT, idn, wait_src, tag):
            w0 = list(wait_src) + [p.g("psT" + tag)]
            pd = src.shape[0]
            for s in range(nsub):
                p.add("tensor",
                      lambda h, s=s: h.transpose(
                          psT[:, s * 128:s * 128 + pd],
                          src[:, s * 128:(s + 1) * 128],
                          idn[:] if pd == 128 else idn[0:pd, 0:pd]),
                      waits=(w0 if s == 0 else []),
                      inc=(("pe", 1) if s == nsub - 1 else None))

        stgr = stg[:].rearrange("p a b -> p (a b)").bitcast(F32R)
        stgr3 = stgr.rearrange("p (a b) -> p a b", a=NDC)

        def w_stream_T(src, writer):
            for c in range(NDC):
                buf = wtbuf[:, c % 2, :]
                w = p.dma("gpsimd", buf, src[c * 128:(c + 1) * 128, :],
                          waits=[p.g("wbufT%d" % (c % 2))])
                pe_T(buf, 8, psTr, idr, [w, ("gs", ldg)], "r")
                p.setg("wbufT%d" % (c % 2), "pe")
                p.add("vector", lambda h: h.tensor_copy(stgr, psTr[:]),
                      waits=[("pe", p.c("pe")), p.g("stg")], inc=("vv", 1))
                p.setg("psTr", "vv")
                writer(c)
                p.setg("stg", "ws")

        def wr_io(dst_d):
            def wr(c):
                p.dma("sync", dst_d[c].rearrange("(kc q) e -> q kc e", q=128),
                      stgr3, waits=[("vv", p.c("vv"))])
            return wr

        def wr_wo(c):
            p.dma("sync", w_outT_d.rearrange("(kc q) e -> q kc e", q=128)
                  [:, :, c * 128:(c + 1) * 128], stgr3,
                  waits=[("vv", p.c("vv"))])

        w_stream_T(w_in_x, wr_io(w_in_xT_d))
        w_stream_T(w_in_z, wr_io(w_in_zT_d))
        w_stream_T(w_out, wr_wo)

        bufx = wtbuf[0:96, 0, :]
        w = p.dma("gpsimd", bufx, w_x, waits=[p.g("wbufT0")])
        pe_T(bufx, 8, psTr, idr, [w], "r")
        p.add("vector", lambda h: h.tensor_copy(
            w_xT[:],
            psTr[:].rearrange("p (a b) -> p a b", a=8)[:, :, 0:96]),
            waits=[("pe", p.c("pe"))], inc=("vv", 1))
        p.setg("psTr", "vv")
        bufd = wtbuf[:, 1, 0:512].rearrange("p (a b) -> p a b", a=NDC)
        w = p.dma("gpsimd", bufd,
                  w_dt.rearrange("(dc q) r -> q dc r", q=128),
                  waits=[p.g("wbufT1")])
        for dc in range(NDC):
            p.add("tensor", lambda h, dc=dc: h.transpose(
                psTr[0:64, dc * 128:(dc + 1) * 128], bufd[:, dc, :], idr[:]),
                waits=([w, p.g("psTr")] if dc == 0 else []),
                inc=(("pe", 1) if dc == NDC - 1 else None))
        p.add("vector", lambda h: h.tensor_copy(
            w_dtT[:].rearrange("p a b -> p (a b)"), psTr[0:64, :]),
            waits=[("pe", p.c("pe"))], inc=("vv", 1))
        p.setg("psTr", "vv")
        p.setg("kxarena", "pe")
        p.setg("wcol0", "pe")
        p.setg("wcol1", "pe")

        def ln_chunk(xch, xnto, ldw, apply_wait):
            nm, m2, v1, sd, rc = (colscr[:, i:i + 1] for i in range(5))
            q0 = sq0[:, 0:1]
            p.add("vector", lambda h: h.tensor_reduce(nm, xch, AX.X, ALU.add),
                  waits=[ldw], inc=("vv", 1))
            p.add("vector", lambda h: h.tensor_scalar(
                nm, nm, -1.0 / DM, None, ALU.mult),
                waits=[("vv", p.c("vv"))], inc=("vv", 1))
            p.add("scalar", lambda h: h.activation(
                lnscr[:], xch, ACT.Square, accum_out=q0),
                waits=[ldw, p.g("lnq")], inc=("ss", 1))
            p.add("vector", lambda h: h.tensor_tensor(m2, nm, nm, ALU.mult),
                  waits=[("ss", p.c("ss")), ("vv", p.c("vv"))], inc=("vv", 1))
            p.add("vector", lambda h: h.tensor_scalar(
                v1, q0, 1.0 / DM, 1e-5, ALU.mult, ALU.add), inc=("vv", 1))
            p.add("vector", lambda h: h.tensor_tensor(
                v1, v1, m2, ALU.subtract),
                waits=[("vv", p.c("vv"))], inc=("vv", 1))
            p.setg("lnq", "vv")
            p.add("scalar", lambda h: h.activation(sd, v1, ACT.Sqrt),
                  waits=[("vv", p.c("vv"))], inc=("ss", 1))
            p.add("vector", lambda h: h.reciprocal(rc, sd),
                  waits=[("ss", p.c("ss"))], inc=("vv", 1))
            p.add("vector", lambda h: h.tensor_scalar(
                xnto, xch, nm, rc, ALU.add, ALU.mult),
                waits=[apply_wait, ("vv", p.c("vv"))], inc=("vv", 1))

        for tc in range(NOC):
            xch = av(0, [128, DM])
            w = p.dma("sync", xch, x_own[tc * 128:(tc + 1) * 128, :],
                      waits=[p.g("kxarena")])
            xnch = av(1024, [128, DM])
            ln_chunk(xch, xnch, w, p.g("kxarena"))
            mx, sx = colscr[:, 6:7], colscr[:, 7:8]
            p.add("vector", lambda h: h.tensor_reduce(
                mx, xnch, AX.X, ALU.max, apply_absolute_value=True),
                inc=("vv", 1))
            p.add("vector", lambda h: h.tensor_scalar(
                mx, mx, 1e-5, None, ALU.max),
                waits=[("vv", p.c("vv"))], inc=("vv", 1))
            p.add("vector", lambda h: h.reciprocal(sx, mx),
                  waits=[("vv", p.c("vv"))], inc=("vv", 1))
            p.add("vector", lambda h: h.tensor_scalar(
                sx, sx, 127.0, None, ALU.mult),
                waits=[("vv", p.c("vv"))], inc=("vv", 1))
            p.add("vector", lambda h, t=tc: h.tensor_scalar(
                rsx[:, t:t + 1], mx, 1.0 / 127.0, None, ALU.mult),
                inc=("vv", 1))
            ki = av(2048, [128, DM], I32)
            kb = av(3072, [128, DM], BF16)
            p.add("vector", lambda h, d=ki, x=xnch: h.tensor_scalar(
                d, x, sx, None, ALU.mult),
                waits=[("vv", p.c("vv"))], inc=("vv", 1))
            p.add("vector", lambda h, d=kb, s2_=ki: h.tensor_copy(d, s2_),
                  inc=("vv", 1))
            pe_T(kb, 8, psTb, idb, [("vv", p.c("vv")), ("gs", ldg)], "b")
            stb = av(3584, [128, NDC, 128], BF16)
            p.add("vector", lambda h, d=stb: h.tensor_copy(
                d.rearrange("p a b -> p (a b)"), psTb[:]),
                waits=[("pe", p.c("pe")), p.g("stbC")], inc=("vv", 1))
            p.setg("psTb", "vv")
            p.dma("sync", kx_d.rearrange("(kc q) t -> q kc t", q=128)
                  [:, :, tc * 128:(tc + 1) * 128], stb,
                  waits=[("vv", p.c("vv"))])
            p.setg("stbC", "ws")
            p.setg("kxarena", "vv")

        for wi, (wsrc, rows, fw) in enumerate(
                ((w_gate, INNER, DM), (w_up, INNER, DM), (w_down, DM, INNER))):
            acc = wsum[:, wi:wi + 1]
            p.add("vector", lambda h, a=acc: h.memset(a, 0.0), inc=("vv", 1))
            ncol = 2 if fw > 2048 else 1
            cw = fw // ncol
            for c in range(rows // 128):
                for j in range(ncol):
                    pp = (c * ncol + j) % 2
                    wch = av(pp * 2048, [128, cw])
                    w = p.dma("sync", wch,
                              wsrc[c * 128:(c + 1) * 128,
                                   j * cw:(j + 1) * cw],
                              waits=[p.g("qbuf%d" % pp)], sem="cq")
                    ab = av(4096, [128, cw])
                    p.add("scalar", lambda h, x=wch, o=ab: h.activation(
                        o, x, ACT.Abs, accum_out=colscr[:, 8:9]),
                        waits=[w, p.g("qabs")], inc=("ss", 1))
                    p.setg("qbuf%d" % pp, "ss")
                    p.add("vector", lambda h, a=acc: h.tensor_tensor(
                        a, a, colscr[:, 8:9], ALU.add),
                        waits=[("ss", p.c("ss")), ("vv", p.c("vv"))],
                        inc=("vv", 1))
                    p.setg("qabs", "vv")
        onescol = colscr[:, 9:10]
        p.add("vector", lambda h: h.memset(onescol, 1.0), inc=("vv", 1))
        ps, (pss, psv), pi = ps_acquire()
        p.add("tensor", lambda h, ps=ps: h.matmul(
            ps[0:1, 0:4], onescol, wsum[:, 0:4], start=True, stop=True),
            waits=[("vv", p.c("vv")), (pss, psv)], inc=("pe", 1))
        p.add("vector", lambda h, ps=ps: h.tensor_copy(
            m4stg[0:1, 0:4], ps[0:1, 0:4]),
            waits=[("pe", p.c("pe"))], inc=("vv", 1))
        ps_release(pi, "vv")
        p.dma("sync", mwrow_d, m4stg[0:1, 0:4], waits=[("vv", p.c("vv"))],
              sem="oq")
        p.dma("gpsimd", mw[:, 0:4], mwrow_d.partition_broadcast(128),
              waits=[("oq", p.c("oq"))])
        ldmw = p.c("gs")
        for wi in range(3):
            p.add("vector", lambda h, wi=wi: h.tensor_scalar(
                mw[:, wi:wi + 1], mw[:, wi:wi + 1], 1.0 / (INNER * DM), 1e-5,
                ALU.mult, ALU.max),
                waits=[("gs", ldmw), ("vv", p.c("vv"))], inc=("vv", 1))
        swcol = colscr[:, 10:11]
        for wi, (wsrc, dstd, rows, fw) in enumerate(
                ((w_gate, gT_g_d, INNER, DM), (w_up, gT_u_d, INNER, DM),
                 (w_down, gdT_d, DM, INNER))):
            p.add("vector", lambda h, wi=wi: h.reciprocal(
                swcol, mw[:, wi:wi + 1]),
                waits=[("vv", p.c("vv"))], inc=("vv", 1))
            swv = ("vv", p.c("vv"))
            ncol = 2 if fw > 2048 else 1
            cw = fw // ncol
            for c in range(rows // 128):
                for j in range(ncol):
                    pp = (c * ncol + j) % 2
                    wch = av(pp * 2048, [128, cw])
                    w = p.dma("sync", wch,
                              wsrc[c * 128:(c + 1) * 128,
                                   j * cw:(j + 1) * cw],
                              waits=[p.g("qbuf%d" % pp)], sem="cq")
                    gi = wch.bitcast(I32)
                    gb = av(6144, [128, cw], BF16)
                    p.add("vector", lambda h, x=wch, d=gi: h.tensor_scalar(
                        d, x, swcol, None, ALU.mult),
                        waits=[w, swv, p.g("qgb")], inc=("vv", 1))
                    p.add("vector", lambda h, s=gi, d=gb: h.tensor_scalar(
                        d, s, -1.0, 1.0, ALU.max, ALU.min), inc=("vv", 1))
                    p.setg("qbuf%d" % pp, "vv")
                    for g in range(cw // 1024):
                        pe_T(gb[:, g * 1024:(g + 1) * 1024], 8, psTb, idb,
                             [("vv", p.c("vv")), ("gs", ldg)], "b")
                        stb = av(8192, [128, NDC, 128], BF16)
                        p.add("vector", lambda h, d=stb: h.tensor_copy(
                            d.rearrange("p a b -> p (a b)"), psTb[:]),
                            waits=[("pe", p.c("pe")), p.g("stbD")],
                            inc=("vv", 1))
                        p.setg("psTb", "vv")
                        if g == cw // 1024 - 1:
                            p.setg("qgb", "vv")
                        if fw == DM:
                            dst = dstd[c].rearrange("(kc q) e -> q kc e", q=128)
                        else:
                            base = (j * 2048 + g * 1024) // 128
                            dst = dstd[c].rearrange(
                                "(kc q) e -> q kc e", q=128)[:, base:base + 8]
                        p.dma("sync", dst, stb, waits=[("vv", p.c("vv"))])
                        p.setg("stbD", "ws")
        p.setg("frontx0", "vv")
        p.setg("frontx1", "vv")
        p.setg("qdone_ss", "ss")
        p.guard["prologue_done"] = ("ws", p.c("ws"))


        def ffn_es_setup(es):
            gcol = av(0, [128, NDC, 512], BF16)
            ucol = av(2048, [128, NDC, 512], BF16)
            for j in range(4):
                w1 = p.dma("sync", gcol[:, :, j * 128:(j + 1) * 128],
                           gT_g_d[es * 4 + j].rearrange(
                               "(kc q) e -> q kc e", q=128),
                           waits=([p.g("gcol"), p.g("prologue_done")]
                                  if j == 0 else []), sem="gq")
            for j in range(4):
                w2 = p.dma("sync", ucol[:, :, j * 128:(j + 1) * 128],
                           gT_u_d[es * 4 + j].rearrange(
                               "(kc q) e -> q kc e", q=128),
                           waits=([p.g("ucol")] if j == 0 else []), sem="gq")
            return gcol, ucol, w2, w2

        def ffn_tb(es, tb, gcol, ucol, w1, w2):
            kxb = kxt[:, tb % 2]
            w3 = p.dma("sync", kxb, kx_d.rearrange(
                "(kc q) t -> q kc t", q=128)
                [:, :, tb * 128:(tb + 1) * 128],
                waits=[p.g("kxt%d" % (tb % 2))], sem="kq%d" % (tb % 2))
            psg, (pgs, pgv), pgi = ps_acquire()
            psu, (pus, puv), pui = ps_acquire()
            for kc in range(NDC):
                p.add("tensor", lambda h, ps=psg, kc=kc, g=gcol, k=kxb:
                      h.matmul(ps[:, 0:512], k[:, kc, :], g[:, kc, :],
                               start=(kc == 0), stop=(kc == NDC - 1)),
                      waits=([w1, w3, (pgs, pgv)] if kc == 0 else []),
                      inc=(("pe", 1) if kc == NDC - 1 else None))
            for kc in range(NDC):
                p.add("tensor", lambda h, ps=psu, kc=kc, g=ucol, k=kxb:
                      h.matmul(ps[:, 0:512], k[:, kc, :], g[:, kc, :],
                               start=(kc == 0), stop=(kc == NDC - 1)),
                      waits=([w2, (pus, puv)] if kc == 0 else []),
                      inc=(("pe", 1) if kc == NDC - 1 else None))
            p.setg("kxt%d" % (tb % 2), "pe")
            p.add("vector", lambda h, tb=tb: h.tensor_tensor(
                colscr[:, 11:12], mw[:, 0:1], rsx[:, tb:tb + 1],
                ALU.mult), waits=[p.g("combo")], inc=("vv", 1))
            gt = av(4096, [128, 512])
            p.add("scalar", lambda h, ps=psg, d=gt: h.activation(
                d, ps[:, 0:512], ACT.Silu, scale=colscr[:, 11:12]),
                waits=[("pe", p.c("pe")), ("vv", p.c("vv")), p.g("gt")],
                inc=("ss", 1))
            p.setg("combo", "ss")
            ps_release(pgi, "ss")
            gu = av(4608, [128, 512])
            p.add("vector", lambda h, ps=psu, g=gt, d=gu: h.tensor_tensor(
                d, g, ps[:, 0:512], ALU.mult),
                waits=[("ss", p.c("ss")), p.g("gu")], inc=("vv", 1))
            p.setg("gt", "vv")
            ps_release(pui, "vv")
            p.add("vector", lambda h, d=gu: h.tensor_reduce(
                colscr[:, 12:13], d, AX.X, ALU.max,
                apply_absolute_value=True), inc=("vv", 1))
            p.add("vector", lambda h, tb=tb: h.tensor_tensor(
                mxgu[:, tb:tb + 1], mxgu[:, tb:tb + 1],
                colscr[:, 12:13], ALU.max),
                waits=[("vv", p.c("vv"))], inc=("vv", 1))
            p.dma("sync", gu_d[tb * 128:(tb + 1) * 128,
                               es * 512:(es + 1) * 512], gu,
                  waits=[("vv", p.c("vv"))], sem="eq")
            p.setg("gu", "eq")

        p.add("vector", lambda h: h.memset(mxgu[:], 0.0), inc=("vv", 1))
        for b in range(NB):
            t0 = b * TBL
            for tc in range(NTC):
                xch = av(8192 + (tc % 2) * 1024, [128, DM])
                w = p.dma("sync", xch,
                          x_full[t0 + tc * 128: t0 + (tc + 1) * 128, :],
                          waits=[p.g("frontx%d" % (tc % 2))],
                          sem="xq%d" % (tc % 2))
                ln_chunk(xch, xn_tm[:], w, p.g("xn_tm"))
                p.setg("frontx%d" % (tc % 2), "vv")
                if debug:
                    p.dma("sync", dr["dbg_xraw"][t0 + tc * 128:
                                                 t0 + (tc + 1) * 128, :],
                          xch, waits=[("vv", p.c("vv"))])
                    p.dma("sync", dr["dbg_xntm"][t0 + tc * 128:
                                                 t0 + (tc + 1) * 128, :],
                          xn_tm[:], waits=[("vv", p.c("vv"))])
                p.setg("frontx", "vv")
                pe_T(xn_tm[:], 8, psTr, idr,
                     [("vv", p.c("vv")), p.g("xnfm_rd"), ("gs", ldg)], "r")
                p.setg("xn_tm", "pe")
                p.add("vector", lambda h, t=tc: h.tensor_copy(
                    xn_fm[:, :, t * 128:(t + 1) * 128],
                    psTr[:].rearrange("p (a b) -> p a b", a=8)),
                    waits=[("pe", p.c("pe"))], inc=("vv", 1))
                p.setg("psTr", "vv")
            xn_ready = ("vv", p.c("vv"))
            if debug:
                p.dma("sync", dr["dbg_xn"].rearrange(
                    "(dc q) t -> q dc t", q=128)[:, :, t0:t0 + TBL],
                    xn_fm[:], waits=[xn_ready])

            for dc in range(NDC):
                wb = wcol[:, dc % 2]
                w = p.dma("sync", wb,
                          w_in_xT_d[dc].rearrange("(kc q) e -> q kc e", q=128),
                          waits=[p.g("wcol%d" % (dc % 2))],
                          sem="wq%d" % (dc % 2))
                ps, (pss, psv), pi = ps_acquire()
                for kc in range(NDC):
                    p.add("tensor", lambda h, ps=ps, kc=kc, wb=wb: h.matmul(
                        ps[:, 0:TBL], wb[:, kc, :], xn_fm[:, kc, :],
                        start=(kc == 0), stop=(kc == NDC - 1)),
                        waits=([w, xn_ready, (pss, psv)] if kc == 0 else []),
                        inc=(("pe", 1) if kc == NDC - 1 else None))
                p.setg("wcol%d" % (dc % 2), "pe")
                xcb = xc[:, dc % 2]
                p.add("vector", lambda h, ps=ps, x=xcb: h.tensor_scalar(
                    x[:, 3:3 + TBL], ps[:, 0:TBL], 0.0, 1.0,
                    ALU.max, ALU.min),
                    waits=[("pe", p.c("pe"))], inc=("vv", 1))
                ps_release(pi, "vv")
                p.add("vector", lambda h, dc=dc, x=xcb: h.tensor_copy(
                    x[:, 0:3], xtail[:, dc, :]), inc=("vv", 1))
                cb = cbuf[:, dc % 2]
                p.add("vector", lambda h, dc=dc, cb=cb, x=xcb: h.tensor_scalar(
                    cb, x[:, 0:TBL], convw[:, dc, 0:1], convb[:, dc, 0:1],
                    ALU.mult, ALU.add),
                    waits=[p.g("cb_m3"), ("ws", ldc), ("vv", p.c("vv"))],
                    inc=("vv", 1))
                for k in range(1, 4):
                    p.add("vector", lambda h, dc=dc, k=k, cb=cb, x=xcb:
                          h.scalar_tensor_tensor(
                              cb, x[:, k:k + TBL], convw[:, dc, k:k + 1],
                              cb, ALU.mult, ALU.add), inc=("vv", 1))
                p.add("vector", lambda h, dc=dc, x=xcb: h.tensor_copy(
                    xtail[:, dc, :], x[:, TBL:TBL + 3]), inc=("vv", 1))
                p.add("scalar", lambda h, dc=dc, cb=cb: h.activation(
                    cbuf[:, (dc + 1) % 2], cb, ACT.Silu),
                    waits=[("vv", p.c("vv")), p.g("cb2%d" % ((dc + 1) % 2))],
                    inc=("ss", 1))
                p.add("vector", lambda h, dc=dc: h.tensor_scalar(
                    u_sb[:, dc, :], cbuf[:, (dc + 1) % 2], 0.0, 1.0,
                    ALU.max, ALU.min),
                    waits=[("ss", p.c("ss")), p.g("u_wr")], inc=("vv", 1))
                p.setg("cb2%d" % ((dc + 1) % 2), "vv")
            u_ready = ("vv", p.c("vv"))
            if debug:
                p.dma("sync", dr["dbg_u"].rearrange(
                    "(dc q) t -> q dc t", q=128)[:, :, t0:t0 + TBL],
                    u_sb[:], waits=[u_ready])

            ps, (pss, psv), pi = ps_acquire()
            for kc in range(NDC):
                p.add("tensor", lambda h, ps=ps, kc=kc: h.matmul(
                    ps[0:96, 0:TBL], w_xT[:, kc, :], u_sb[:, kc, :],
                    start=(kc == 0), stop=(kc == NDC - 1)),
                    waits=([u_ready, (pss, psv)] if kc == 0 else []),
                    inc=(("pe", 1) if kc == NDC - 1 else None))
            p.add("vector", lambda h, ps=ps: h.tensor_copy(
                xdstg[:], ps[0:96, 0:TBL]),
                waits=[("pe", p.c("pe")), p.g("xdstg")], inc=("vv", 1))
            ps_release(pi, "vv")
            p.dma("sync", ar_in_d[b], xdstg[:], waits=[("vv", p.c("vv"))])
            p.setg("xdstg", "ws")
            p.add("gpsimd", lambda h, b=b: h.collective_compute(
                "AllReduce", ALU.add, replica_groups=RG,
                ins=[ar_in_d[b]], outs=[ar_out_d[b]]),
                waits=[("ws", p.c("ws"))], inc=("cc", 1))
            cc_b = ("cc", p.c("cc"))
            w = p.dma("sync", xdbl[:], ar_out_d[b],
                      waits=[cc_b, p.g("xdbl")], sem="bq")
            if debug:
                p.dma("sync", dr["dbg_xdbl"][b], xdbl[:], waits=[w])
            p.add("vector", lambda h: h.tensor_scalar(
                dtraw[:], xdbl[0:64, :], 0.0, 1.0, ALU.max, ALU.min),
                waits=[w, p.g("dtraw_rd")], inc=("vv", 1))
            dtraw_ready = ("vv", p.c("vv"))
            p.setg("xdbl", "vv")

            for dc in range(NDC):
                ps, (pss, psv), pi = ps_acquire()
                p.add("tensor", lambda h, ps=ps, dc=dc: h.matmul(
                    ps[:, 0:TBL], w_dtT[:, dc, :], dtraw[:],
                    start=True, stop=True),
                    waits=[dtraw_ready, (pss, psv)], inc=("pe", 1))
                if dc == NDC - 1:
                    p.setg("dtraw_rd", "pe")
                p.add("scalar", lambda h, ps=ps, dc=dc: h.activation(
                    cbuf[:, 0], ps[:, 0:TBL], ACT.Exp, bias=bdt[:, dc, :]),
                    waits=[("pe", p.c("pe")), p.g("cb0"), ("ws", ldc)],
                    inc=("ss", 1))
                ps_release(pi, "ss")
                p.add("scalar", lambda h: h.activation(
                    cbuf[:, 1], cbuf[:, 0], ACT.Ln, bias=1.0), inc=("ss", 1))
                p.setg("cb0", "ss")
                p.add("vector", lambda h, dc=dc: h.tensor_scalar(
                    dtc[:, dc, :], cbuf[:, 1], 1e-4, 20.0, ALU.max, ALU.min),
                    waits=[("ss", p.c("ss"))], inc=("vv", 1))
                p.add("vector", lambda h, dc=dc: h.tensor_tensor(
                    dtu[:, dc, :], dtc[:, dc, :], u_sb[:, dc, :], ALU.mult),
                    inc=("vv", 1))
            dt_ready = ("vv", p.c("vv"))
            p.setg("cb_m3", "ss")
            if debug:
                p.dma("sync", dr["dbg_dt"].rearrange(
                    "(dc q) t -> q dc t", q=128)[:, :, t0:t0 + TBL], dtc[:],
                    waits=[dt_ready])

            gcol_b, ucol_b, wg1, wg2 = ffn_es_setup(b)
            yacc = yacc_sb[:]
            p.add("vector", lambda h, y=yacc: h.memset(
                y.rearrange("p a b -> p (a b)"), 0.0), inc=("vv", 1))
            ffn_tb(b, 0, gcol_b, ucol_b, wg1, wg2)
            ffn_tb(b, 1, gcol_b, ucol_b, wg1, wg2)
            for n in range(NST):
                nb_ = n % 2
                w1 = p.dma("gpsimd", brep[:, nb_],
                           ar_out_d[b, DTR + n:DTR + n + 1, :]
                           .partition_broadcast(128),
                           waits=[cc_b, p.g("brep%d" % nb_)])
                w2 = p.dma("gpsimd", crep[:, nb_],
                           ar_out_d[b, DTR + NST + n:DTR + NST + n + 1, :]
                           .partition_broadcast(128),
                           waits=[p.g("crep%d" % nb_)])
                for dc in range(NDC):
                    i = dc % 2
                    p.add("scalar", lambda h, dc=dc, n=n, i=i: h.activation(
                        dA_sb[:, i], dtc[:, dc, :], ACT.Exp,
                        scale=negA[:, dc, n:n + 1]),
                        waits=[dt_ready, p.g("dA%d" % i)], inc=("ss", 1))
                    sx_ = ("ss", p.c("ss"))
                    p.add("vector", lambda h, dc=dc, i=i, nb_=nb_:
                          h.tensor_tensor(dBu_sb[:, i], dtu[:, dc, :],
                                          brep[:, nb_], ALU.mult),
                          waits=[w2, sx_], inc=("vv", 1))
                    p.add("vector", lambda h, dc=dc, n=n, i=i:
                          h.tensor_tensor_scan(
                              hseq_sb[:, i], dA_sb[:, i], dBu_sb[:, i],
                              hst[:, dc, n:n + 1], ALU.mult, ALU.add),
                          waits=[p.g("hstc%d" % i)], inc=("vv", 1))
                    p.setg("dA%d" % i, "vv")
                    p.add("vector", lambda h, dc=dc, i=i, nb_=nb_:
                          h.tensor_tensor(cstage[:, dc, :], hseq_sb[:, i],
                                          crep[:, nb_], ALU.mult),
                          waits=[w2], inc=("vv", 1))
                    p.add("vector", lambda h, dc=dc, y=yacc:
                          h.tensor_tensor(y[:, dc, :], y[:, dc, :],
                                          cstage[:, dc, :], ALU.add),
                          inc=("vv", 1))
                    p.add("scalar", lambda h, dc=dc, n=n, i=i: h.activation(
                        hst[:, dc, n:n + 1], hseq_sb[:, i, TBL - 1:TBL],
                        ACT.Copy), waits=[("vv", p.c("vv"))], inc=("ss", 1))
                    p.setg("hstc%d" % i, "ss")
                p.setg("brep%d" % nb_, "vv")
                p.setg("crep%d" % nb_, "vv")
                if n < NST - 2:
                    ffn_tb(b, n + 2, gcol_b, ucol_b, wg1, wg2)
            p.setg("gcol", "pe")
            p.setg("ucol", "pe")
            scan_done = ("vv", p.c("vv"))
            if debug:
                p.dma("sync", dr["dbg_ys"].rearrange(
                    "(dc q) t -> q dc t", q=128)[:, :, t0:t0 + TBL],
                    yacc, waits=[scan_done])

            for dc in range(NDC):
                wb = wcol[:, dc % 2]
                w = p.dma("sync", wb,
                          w_in_zT_d[dc].rearrange("(kc q) e -> q kc e", q=128),
                          waits=[p.g("wcol%d" % (dc % 2))],
                          sem="wq%d" % (dc % 2))
                ps, (pss, psv), pi = ps_acquire()
                for kc in range(NDC):
                    p.add("tensor", lambda h, ps=ps, kc=kc, wb=wb: h.matmul(
                        ps[:, 0:TBL], wb[:, kc, :], xn_fm[:, kc, :],
                        start=(kc == 0), stop=(kc == NDC - 1)),
                        waits=([w, (pss, psv)] if kc == 0 else []),
                        inc=(("pe", 1) if kc == NDC - 1 else None))
                p.setg("wcol%d" % (dc % 2), "pe")
                p.add("vector", lambda h, ps=ps: h.tensor_scalar(
                    zscr[:, 0], ps[:, 0:TBL], 0.0, 1.0, ALU.max, ALU.min),
                    waits=[("pe", p.c("pe")), p.g("z0")], inc=("vv", 1))
                ps_release(pi, "vv")
                p.add("scalar", lambda h: h.activation(
                    zscr[:, 1], zscr[:, 0], ACT.Silu),
                    waits=[("vv", p.c("vv")), p.g("z1")], inc=("ss", 1))
                sgate = ("ss", p.c("ss"))
                p.add("vector", lambda h, dc=dc, y=yacc: h.scalar_tensor_tensor(
                    zscr[:, 0], u_sb[:, dc, :], dvec[:, dc, :], y[:, dc, :],
                    ALU.mult, ALU.add),
                    waits=[scan_done, ("ws", ldc), sgate],
                    inc=("vv", 1))
                p.add("vector", lambda h: h.tensor_scalar(
                    zscr[:, 0], zscr[:, 0], 0.0, 1.0, ALU.max, ALU.min),
                    inc=("vv", 1))
                p.add("vector", lambda h: h.tensor_tensor(
                    zscr[:, 0], zscr[:, 0], zscr[:, 1], ALU.mult),
                    waits=[sgate], inc=("vv", 1))
                p.setg("z1", "vv")
                p.add("vector", lambda h, dc=dc: h.tensor_scalar(
                    u_sb[:, dc, :], zscr[:, 0], 0.0, 1.0, ALU.max, ALU.min),
                    inc=("vv", 1))
                p.setg("z0", "vv")
            y_ready = ("vv", p.c("vv"))
            p.setg("yacc_arena", "vv")
            if debug:
                p.dma("sync", dr["dbg_y"].rearrange(
                    "(dc q) t -> q dc t", q=128)[:, :, t0:t0 + TBL],
                    u_sb[:], waits=[y_ready])

            woutb = xn_fm[:].rearrange("p a b -> p (a b)")
            wv = woutb.rearrange("p (a b) -> p a b", a=NDC)
            for half in range(2):
                w = p.dma("sync", wv,
                          w_outT_d.rearrange("(kc q) e -> q kc e", q=128)
                          [:, :, half * 512:(half + 1) * 512],
                          waits=([y_ready, p.g("woutb")] if half == 0
                                 else [p.g("woutb")]), sem="vq")
                for ts in range(NTC):
                    ps, (pss, psv), pi = ps_acquire()
                    for kc in range(NDC):
                        p.add("tensor", lambda h, ps=ps, ts=ts, kc=kc:
                              h.matmul(
                                  ps[:, 0:512],
                                  u_sb[:, kc, ts * 128:(ts + 1) * 128],
                                  wv[:, kc, :],
                                  start=(kc == 0), stop=(kc == NDC - 1)),
                              waits=([w, y_ready, (pss, psv)]
                                     if kc == 0 else []),
                              inc=(("pe", 1) if kc == NDC - 1 else None))
                    p.add("scalar", lambda h, ps=ps, half=half:
                          h.activation(m4stg[:, half * 512:(half + 1) * 512],
                                       ps[:, 0:512], ACT.Copy),
                          waits=[("pe", p.c("pe")), p.g("m4s%d" % half)],
                          inc=("ss", 1))
                    ps_release(pi, "ss")
                    p.dma("sync",
                          m4p_d[t0 + ts * 128: t0 + (ts + 1) * 128,
                                half * 512:(half + 1) * 512],
                          m4stg[:, half * 512:(half + 1) * 512],
                          waits=[("ss", p.c("ss"))], sem="mq")
                    if debug:
                        p.dma("sync",
                              dr["dbg_m4p"][t0 + ts * 128: t0 + (ts + 1) * 128,
                                            half * 512:(half + 1) * 512],
                              m4stg[:, half * 512:(half + 1) * 512])
                    p.setg("m4s%d" % half, "mq")
                p.setg("woutb", "pe")
            p.setg("xnfm_rd", "pe")
            p.setg("u_wr", "pe")

        p.add("gpsimd", lambda h: h.collective_compute(
            "ReduceScatter", ALU.add, replica_groups=RG,
            ins=[m4p_d], outs=[rs_out_d]),
            waits=[("ws", p.c("ws")), ("mq", p.c("mq"))], inc=("cc", 1))
        cc_rs = ("cc", p.c("cc"))

        if debug:
            for ch in range(8):
                wmp = p.dma("sync", m4stg[:],
                            m4p_d[ch * 512:ch * 512 + 128, :],
                            waits=[cc_rs, ("ss", p.c("ss"))])
                p.dma("sync", dr["dbg_m4post"][ch], m4stg[:], waits=[wmp])
        blocks_ws = ("ws", p.c("ws"))
        blocks_pe = ("pe", p.c("pe"))
        for tb in range(NTB):
            guin = av((tb % 2) * 4096, [128, INNER])
            w = p.dma("sync", guin, gu_d[tb * 128:(tb + 1) * 128, :],
                      waits=[p.g("guin%d" % (tb % 2)), blocks_pe, p.g("gu")]
                      if tb <= 1 else [p.g("guin%d" % (tb % 2))], sem="uq")
            ac, ia, th, mxc, s2, bt = (colscr[:, i:i + 1]
                                       for i in range(13, 19))
            p.add("vector", lambda h, tb=tb: h.tensor_tensor(
                ac, mw[:, 1:2], rsx[:, tb:tb + 1], ALU.mult), inc=("vv", 1))
            p.add("vector", lambda h: h.reciprocal(ia, ac),
                  waits=[("vv", p.c("vv"))], inc=("vv", 1))
            p.add("vector", lambda h: h.tensor_scalar(
                th, ia, 1e-5, None, ALU.mult),
                waits=[("vv", p.c("vv"))], inc=("vv", 1))
            p.add("vector", lambda h, tb=tb: h.tensor_tensor(
                mxc, mxgu[:, tb:tb + 1], th, ALU.max),
                waits=[("vv", p.c("vv"))], inc=("vv", 1))
            p.add("vector", lambda h: h.reciprocal(s2, mxc),
                  waits=[("vv", p.c("vv"))], inc=("vv", 1))
            p.add("vector", lambda h: h.tensor_scalar(
                s2, s2, 127.0, None, ALU.mult),
                waits=[("vv", p.c("vv"))], inc=("vv", 1))
            p.add("vector", lambda h: h.tensor_tensor(
                bt, mw[:, 2:3], ac, ALU.mult), inc=("vv", 1))
            p.add("vector", lambda h: h.tensor_tensor(
                bt, bt, mxc, ALU.mult),
                waits=[("vv", p.c("vv"))], inc=("vv", 1))
            p.add("vector", lambda h: h.tensor_scalar(
                bt, bt, 1.0 / 127.0, None, ALU.mult),
                waits=[("vv", p.c("vv"))], inc=("vv", 1))
            p.dma("sync",
                  beta_d[:, tb * 128:(tb + 1) * 128].rearrange("a b -> b a"),
                  bt, waits=[("vv", p.c("vv"))], sem="eq")
            ki = guin.bitcast(I32)
            kb = av(8192, [128, INNER], BF16)
            p.add("vector", lambda h, s=guin, d=ki: h.tensor_scalar(
                d, s, s2, None, ALU.mult),
                waits=([w, ("vv", p.c("vv")), blocks_ws] if tb == 0
                       else [w, ("vv", p.c("vv"))]), inc=("vv", 1))
            p.add("vector", lambda h, s=ki, d=kb: h.tensor_copy(d, s),
                  waits=[p.g("kbq")], inc=("vv", 1))
            p.setg("guin%d" % (tb % 2), "vv")
            for g in range(4):
                pe_T(kb[:, g * 1024:(g + 1) * 1024], 8, psTb, idb,
                     [("vv", p.c("vv")), ("gs", ldg)], "b")
                stb = av(6656, [128, NDC, 128], BF16)
                p.add("vector", lambda h, d=stb: h.tensor_copy(
                    d.rearrange("p a b -> p (a b)"), psTb[:]),
                    waits=[("pe", p.c("pe")), p.g("stbq")], inc=("vv", 1))
                p.setg("psTb", "vv")
                if g == 3:
                    p.setg("kbq", "vv")
                p.dma("sync", kgu_d.rearrange(
                    "(g kc q) t -> q g kc t", q=128, kc=8)[:, g, :,
                    tb * 128:(tb + 1) * 128], stb,
                    waits=[("vv", p.c("vv"))], sem="sq")
                p.setg("stbq", "sq")
                p.setg("kgu_st", "sq")

        beta_ws = ("eq", p.c("eq"))
        ffn_ws = ("ws", p.c("ws"))

        def final_tc(tc):
            a1 = zscr[:].rearrange("p a b -> p (a b)")
            a2 = hseq_sb[:].rearrange("p a b -> p (a b)")
            a3 = dBu_sb[:].rearrange("p a b -> p (a b)")
            wa = p.dma("scalar", a1, rs_out_d[tc * 128:(tc + 1) * 128, :],
                       waits=[cc_rs, p.g("fin"), p.g("fa")], sem="fq")
            p.dma("scalar", a2, ffn_d[tc * 128:(tc + 1) * 128, :],
                  waits=[ffn_ws] if tc % 4 == 0 else [], sem="fq")
            p.dma("scalar", a3, x_own[tc * 128:(tc + 1) * 128, :], sem="fq")
            ldf = ("fq", p.c("fq"))
            p.add("vector", lambda h, a=a1, c2=a2: h.tensor_tensor(
                a, a, c2, ALU.add), waits=[ldf], inc=("vv", 1))
            p.add("vector", lambda h, a=a1, c3=a3: h.tensor_tensor(
                a, a, c3, ALU.add), inc=("vv", 1))
            p.setg("fa", "vv")
            p.dma("scalar", out[tc * 128:(tc + 1) * 128, :], a1,
                  waits=[("vv", p.c("vv"))], sem="oq")
            p.setg("fin", "oq")

        for ts in range(NTS):
            kts0 = av(0, [128, 16, 512], BF16)
            kts1 = av(4096, [128, 16, 512], BF16)
            wk0 = p.dma("sync", kts0, kgu_d.rearrange(
                "(es q) t -> q es t", q=128)[:, 0:16,
                ts * 512:(ts + 1) * 512],
                waits=[p.g("ktsh0"), p.g("guin0"), p.g("guin1"),
                       p.g("kgu_st")] if ts == 0 else [p.g("ktsh0")],
                sem="tq")
            wk1 = p.dma("sync", kts1, kgu_d.rearrange(
                "(es q) t -> q es t", q=128)[:, 16:32,
                ts * 512:(ts + 1) * 512],
                waits=[p.g("ktsh1")], sem="tq")
            brp = brep[:, 0]
            w2 = p.dma("gpsimd", brp,
                       beta_d[:, ts * 512:(ts + 1) * 512]
                       .partition_broadcast(128),
                       waits=[beta_ws, p.g("brep0"), p.g("brp")])
            for ec in range(NDC):
                gdc = av(8192, [128, 32, 128], BF16)
                w3 = p.dma("sync", gdc, gdT_d[ec].rearrange(
                    "(es q) e -> q es e", q=128),
                    waits=([p.g("gdc"), ("vv", p.c("vv")), ("ss", p.c("ss"))]
                           if (ts == 0 and ec == 0) else [p.g("gdc")]),
                    sem="dq")
                ps, (pss, psv), pi = ps_acquire()
                for es in range(32):
                    kbuf = kts0 if es < 16 else kts1
                    p.add("tensor", lambda h, ps=ps, es=es, g=gdc, k=kbuf:
                          h.matmul(ps[:, 0:512], g[:, es, :],
                                   k[:, es % 16, :],
                                   start=(es == 0), stop=(es == 31)),
                          waits=([wk0, w3, (pss, psv)] if es == 0 else
                                 ([wk1] if es == 16 else [])),
                          inc=(("pe", 1) if es == 31 else None))
                p.setg("gdc", "pe")
                if ec == NDC - 1:
                    p.setg("ktsh0", "pe")
                    p.setg("ktsh1", "pe")
                fsc = crep[:, 0]
                p.add("vector", lambda h, ps=ps, d=fsc: h.tensor_tensor(
                    d, ps[:, 0:512], brp, ALU.mult),
                    waits=[("pe", p.c("pe")), w2, p.g("fsc")], inc=("vv", 1))
                ps_release(pi, "vv")
                ps2_, (p2s, p2v), pi2 = ps_acquire()
                for sx2 in range(4):
                    p.add("tensor", lambda h, s=sx2, src=fsc, ps=ps2_:
                          h.transpose(ps[:, s * 128:(s + 1) * 128],
                                      src[:, s * 128:(s + 1) * 128], idf[:]),
                          waits=([("vv", p.c("vv")), (p2s, p2v), ("ws", ldc)]
                                 if sx2 == 0 else []),
                          inc=(("pe", 1) if sx2 == 3 else None))
                p.setg("fsc", "pe")
                ftm = dA_sb[:, 0].rearrange("p (a b) -> p a b", a=4)
                p.add("vector", lambda h, d=ftm, ps=ps2_: h.tensor_copy(
                    d.rearrange("p a b -> p (a b)"), ps[:, 0:512]),
                    waits=[("pe", p.c("pe")), p.g("ftm")], inc=("vv", 1))
                ps_release(pi2, "vv")
                p.dma("sync", ffn_d.rearrange(
                    "(sq q) e -> q sq e", q=128)[:, ts * 4:ts * 4 + 4,
                    ec * 128:(ec + 1) * 128], ftm,
                    waits=[("vv", p.c("vv"))])
                p.setg("ftm", "ws")
            p.setg("brp", "vv")
            ffn_ws = ("ws", p.c("ws"))
            if ts > 0:
                for tc in range((ts - 1) * 4, ts * 4):
                    final_tc(tc)
        for tc in range((NTS - 1) * 4, NOC):
            final_tc(tc)
        fin = p.c("oq")
        p.add("sync", lambda h: None, waits=[("oq", fin)])
        p.add("scalar", lambda h: None, waits=[("oq", fin)])

        names = set(p.cnt)
        for ops in p.ops.values():
            for _, waits, inc in ops:
                for nm_, _v in waits:
                    names.add(nm_)
                if inc:
                    names.add(inc[0])
        sems = {nm_: sem(nm_) for nm_ in sorted(names)}
        p.emit(block, sems)
    return nc, dr


def make_in_maps(inputs):
    eye = np.eye(128, dtype=np.float32)
    maps = []
    for c in range(8):
        b, m = c >> 1, c & 1
        dh = slice(m * DH, (m + 1) * DH)
        maps.append({
            "x_full": np.ascontiguousarray(inputs["L0_attn_hidden_states"][b]),
            "x_own": np.ascontiguousarray(
                inputs["L0_attn_hidden_states"][b][m * TOWN:(m + 1) * TOWN]),
            "w_in_x": np.ascontiguousarray(inputs["L0_attn_W_in"][dh]),
            "w_in_z": np.ascontiguousarray(
                inputs["L0_attn_W_in"][2048 + m * DH: 2048 + (m + 1) * DH]),
            "conv_w": np.ascontiguousarray(inputs["L0_attn_conv_w"][dh, 0, :]),
            "conv_b": np.ascontiguousarray(
                inputs["L0_attn_conv_b"][dh].reshape(DH, 1)),
            "w_x": np.ascontiguousarray(inputs["L0_attn_W_x"][:, dh]),
            "w_dt": np.ascontiguousarray(inputs["L0_attn_W_dt"][dh]),
            "b_dt": np.ascontiguousarray(
                inputs["L0_attn_b_dt"][dh].reshape(DH, 1)),
            "w_out": np.ascontiguousarray(inputs["L0_attn_W_out"][:, dh]),
            "a_log": np.ascontiguousarray(inputs["L0_attn_A_log"][dh]),
            "d_vec": np.ascontiguousarray(
                inputs["L0_attn_D"][dh].reshape(DH, 1)),
            "w_gate": inputs["L0_ffn_w_gate"],
            "w_up": inputs["L0_ffn_w_up"],
            "w_down": inputs["L0_ffn_w_down"],
            "ident": eye,
        })
    return maps


def kernel(**inputs):
    nc, _ = build_program()
    maps = make_in_maps(inputs)
    res = run_bass_kernel_spmd(nc, maps, core_ids=list(range(8)))
    outa = np.empty((B, T, DM), dtype=np.float32)
    for c in range(8):
        b, m = c >> 1, c & 1
        outa[b, m * TOWN:(m + 1) * TOWN] = res.results[c]["out"]
    return outa


# version 12
# speedup vs baseline: 1.1305x; 1.1305x over previous
import numpy as np
from contextlib import ExitStack
import concourse.bass as bass
import concourse.mybir as mybir
from concourse.bass_utils import run_bass_kernel_spmd

F32 = mybir.dt.float32
F32R = mybir.dt.float32r
BF16 = mybir.dt.bfloat16
I32 = mybir.dt.int32
ALU = mybir.AluOpType
ACT = mybir.ActivationFunctionType
AX = mybir.AxisListType

B, T, DM = 4, 4096, 1024
DH, NST, DTR = 1024, 16, 64
TBL = 512
NB = T // TBL
NTC = TBL // 128
NDC = DH // 128
TOWN = T // 2
NOC = TOWN // 128
INNER = 4096
NES = INNER // 512
NTB = TOWN // 128
NTS = 4
RG = [[0, 1], [2, 3], [4, 5], [6, 7]]

DEBUG = False


class Prog:
    ENGINES = ("sync", "gpsimd", "vector", "scalar", "tensor")

    def __init__(self):
        self.ops = {e: [] for e in self.ENGINES}
        self.cnt = {}
        self.guard = {}

    def c(self, sem):
        return self.cnt.get(sem, 0)

    def add(self, eng, builder, waits=(), inc=None):
        waits = tuple((s, v) for (s, v) in waits if v > 0)
        if inc is not None:
            self.cnt[inc[0]] = self.c(inc[0]) + inc[1]
        self.ops[eng].append((builder, waits, inc))

    def dma(self, eng, out, in_, waits=(), sem=None, slow=False):
        semn = sem or ("gs" if eng == "gpsimd" else "ws")
        self.add(eng, lambda h, o=out, i=in_: h.dma_start(
            out=o, in_=i, allow_slow_non_contiguous=slow),
            waits=waits, inc=(semn, 16))
        return (semn, self.c(semn))

    def setg(self, name, sem):
        self.guard[name] = (sem, self.c(sem))

    def g(self, name):
        return self.guard.get(name, ("vv", 0))

    def emit(self, block, sems):
        for eng in self.ENGINES:
            ops = list(self.ops[eng])

            def mk(ops):
                def run(h):
                    for builder, waits, inc in ops:
                        for s, v in waits:
                            h.wait_ge(sems[s], v)
                        inst = builder(h)
                        if inc is not None:
                            inst.then_inc(sems[inc[0]], inc[1])
                return run
            getattr(block, eng)(mk(ops))


def build_program(debug=DEBUG):
    nc = bass.Bass("TRN2", target_bir_lowering=False)
    p = Prog()
    dr = {}

    def din(name, shape, dtype=F32):
        dr[name] = nc.dram_tensor(name, list(shape), dtype,
                                  kind="ExternalInput").ap()
        return dr[name]

    def dout(name, shape, dtype=F32):
        dr[name] = nc.dram_tensor(name, list(shape), dtype,
                                  kind="ExternalOutput").ap()
        return dr[name]

    _CC = {"ar_in_d", "ar_out_d", "m4p_d", "rs_out_d"}

    def dint(name, shape, dtype=F32):
        kind = "ExternalOutput" if (debug and name not in _CC) else "Internal"
        dr[name] = nc.dram_tensor(name, list(shape), dtype, kind=kind).ap()
        return dr[name]

    x_full = din("x_full", [T, DM])
    x_own = din("x_own", [TOWN, DM])
    w_in_x = din("w_in_x", [DH, DM])
    w_in_z = din("w_in_z", [DH, DM])
    conv_w = din("conv_w", [DH, 4])
    conv_b = din("conv_b", [DH, 1])
    w_x = din("w_x", [96, DH])
    w_dt = din("w_dt", [DH, DTR])
    b_dt = din("b_dt", [DH, 1])
    w_out = din("w_out", [DM, DH])
    a_log = din("a_log", [DH, NST])
    d_vec = din("d_vec", [DH, 1])
    w_gate = din("w_gate", [INNER, DM])
    w_up = din("w_up", [INNER, DM])
    w_down = din("w_down", [DM, INNER])
    ident = din("ident", [128, 128])
    out = dout("out", [TOWN, DM])

    w_in_xT_d = dint("w_in_xT_d", [NDC, DM, 128], F32R)
    w_in_zT_d = dint("w_in_zT_d", [NDC, DM, 128], F32R)
    w_outT_d = dint("w_outT_d", [DM, DM], F32R)
    gT_g_d = dint("gT_g_d", [32, DM, 128], BF16)
    gT_u_d = dint("gT_u_d", [32, DM, 128], BF16)
    gdT_d = dint("gdT_d", [NDC, INNER, 128], BF16)
    kx_d = dint("kx_d", [DM, TOWN], BF16)
    ar_in_d = dint("ar_in_d", [NB, 96, TBL])
    ar_out_d = dint("ar_out_d", [NB, 96, TBL])
    m4p_d = dint("m4p_d", [T, DM])
    rs_out_d = dint("rs_out_d", [TOWN, DM])
    gu_d = dint("gu_d", [TOWN, INNER])
    kgu_d = dint("kgu_d", [INNER, TOWN], BF16)
    ffn_d = dint("ffn_d", [TOWN, DM])
    beta_d = dint("beta_d", [1, TOWN])
    mwrow_d = dint("mwrow_d", [1, 4])
    if debug:
        dint("dbg_xntm", [T, DM], F32R)
        dint("dbg_ys", [DH, T])
        dint("dbg_m4post", [8, 128, DM])
        dint("dbg_xraw", [T, DM])
        dint("dbg_xdbl", [NB, 96, TBL])
        dint("dbg_m4p", [T, DM])
        dint("dbg_rs", [TOWN, DM])
        dint("dbg_u", [DH, T], F32R)
        dint("dbg_dt", [DH, T])
        dint("dbg_y", [DH, T], F32R)
        dint("dbg_xn", [DM, T], F32R)

    AR_W = 10240

    with ExitStack() as _st:
        def sb(name, shape, dtype):
            return _st.enter_context(nc.sbuf_tensor(name, shape, dtype))

        def pt(name, shape, dtype):
            return _st.enter_context(nc.psum_tensor(name, shape, dtype))

        def sem(name):
            return _st.enter_context(nc.semaphore(name))

        w_xT = sb("w_xT", [128, NDC, 96], F32R)
        w_dtT = sb("w_dtT", [64, NDC, 128], F32R)
        negA = sb("negA", [128, NDC, NST], F32)
        convw = sb("convw", [128, NDC, 4], F32)
        convb = sb("convb", [128, NDC, 1], F32)
        bdt = sb("bdt", [128, NDC, 1], F32)
        dvec = sb("dvec", [128, NDC, 1], F32)
        hst = sb("hst", [128, NDC, NST], F32)
        rsx = sb("rsx", [128, NOC], F32)
        mxgu = sb("mxgu", [128, NTB], F32)
        idf = sb("idf", [128, 128], F32)
        idr = sb("idr", [128, 128], F32R)
        idb = sb("idb", [128, 128], BF16)
        mw = sb("mw", [128, 4], F32)
        colscr = sb("colscr", [128, 19], F32)
        u_sb = sb("u_sb", [128, NDC, TBL], F32R)
        dtc = sb("dtc", [128, NDC, TBL], F32)
        dtu = sb("dtu", [128, NDC, TBL], F32)
        xn_fm = sb("xn_fm", [128, NDC, TBL], F32R)
        xn_tm = sb("xn_tm", [128, DM], F32R)
        xc = sb("xc", [128, 2, TBL + 3], F32)
        xtail = sb("xtail", [128, NDC, 3], F32)
        cstage = sb("cstage", [128, NDC, TBL], F32)
        cbuf = sb("cbuf", [128, 2, TBL], F32)
        wcol = sb("wcol", [128, 2, NDC, 128], F32R)
        kxt = sb("kxt", [128, 2, NDC, 128], BF16)
        dA_sb = sb("dA", [128, 2, TBL], F32)
        dBu_sb = sb("dBu", [128, 2, TBL], F32)
        hseq_sb = sb("hseq", [128, 2, TBL], F32)
        brep = sb("brep", [128, 2, TBL], F32)
        crep = sb("crep", [128, 2, TBL], F32)
        xdbl = sb("xdbl", [96, TBL], F32)
        xdstg = sb("xdstg", [96, TBL], F32)
        dtraw = sb("dtraw", [64, TBL], F32R)
        stg = sb("stg", [128, NDC, 128], F32)
        arena = sb("arena", [128, AR_W], F32)
        sq0 = sb("sq0", [128, 2], F32)
        yacc_sb = sb("yacc_sb", [128, NDC, TBL], F32)
        ps0 = pt("ps0", [128, 512], F32)
        ps1 = pt("ps1", [128, 512], F32)
        ps2 = pt("ps2", [128, 512], F32)
        ps3 = pt("ps3", [128, 512], F32)
        psTb = pt("psTb", [128, 1024], BF16)
        psTr = pt("psTr", [128, 1024], F32R)
        block = _st.enter_context(nc.Block())

        PS = [ps0, ps1, ps2, ps3]
        ps_state = [("vv", 0)] * 4
        ps_rr = [0]

        def ps_acquire():
            i = ps_rr[0] % 4
            ps_rr[0] += 1
            return PS[i], ps_state[i], i

        def ps_release(i, sem):
            ps_state[i] = (sem, p.c(sem))

        def av(off, shape, dtype=F32):
            n = 1
            for s in shape[1:]:
                n *= s
            words = n * mybir.dt.size(dtype) // 4
            assert off + words <= AR_W, (off, words)
            v = arena[:, off:off + words]
            if dtype != F32:
                v = v.bitcast(dtype)
            if len(shape) == 3:
                v = v.rearrange("p (a b) -> p a b", a=shape[1])
            return v

        lnscr = av(6144, [128, DM])
        wsum = colscr[:, 13:17]
        zscr = cbuf
        wtbuf = wcol[:].rearrange("p a b c -> p a (b c)")
        m4stg = av(5120, [128, DM])
        p.dma("sync", idf[:], ident)
        p.dma("gpsimd", idr[:], ident)
        p.dma("gpsimd", idb[:], ident)
        r3 = lambda ap: ap.rearrange("(dc q) n -> q dc n", q=128)
        p.dma("sync", convw[:], r3(conv_w))
        p.dma("sync", convb[:], r3(conv_b), slow=True)
        p.dma("sync", bdt[:], r3(b_dt), slow=True)
        p.dma("sync", dvec[:], r3(d_vec), slow=True)
        p.dma("sync", negA[:], r3(a_log))
        ldc = p.c("ws")
        ldg = p.c("gs")
        p.add("scalar", lambda h: h.activation(negA[:], negA[:], ACT.Exp),
              waits=[("ws", ldc)], inc=("ss", 1))
        p.add("vector", lambda h: h.tensor_scalar(
            negA[:], negA[:], -1.0, None, ALU.mult),
            waits=[("ss", p.c("ss"))], inc=("vv", 1))
        p.add("vector", lambda h: h.memset(hst[:], 0.0), inc=("vv", 1))
        p.add("vector", lambda h: h.memset(xtail[:], 0.0), inc=("vv", 1))

        def pe_T(src, nsub, psT, idn, wait_src, tag):
            w0 = list(wait_src) + [p.g("psT" + tag)]
            pd = src.shape[0]
            for s in range(nsub):
                p.add("tensor",
                      lambda h, s=s: h.transpose(
                          psT[:, s * 128:s * 128 + pd],
                          src[:, s * 128:(s + 1) * 128],
                          idn[:] if pd == 128 else idn[0:pd, 0:pd]),
                      waits=(w0 if s == 0 else []),
                      inc=(("pe", 1) if s == nsub - 1 else None))

        stgr = stg[:].rearrange("p a b -> p (a b)").bitcast(F32R)
        stgr3 = stgr.rearrange("p (a b) -> p a b", a=NDC)

        def w_stream_T(src, writer):
            for c in range(NDC):
                buf = wtbuf[:, c % 2, :]
                w = p.dma("gpsimd", buf, src[c * 128:(c + 1) * 128, :],
                          waits=[p.g("wbufT%d" % (c % 2))])
                pe_T(buf, 8, psTr, idr, [w, ("gs", ldg)], "r")
                p.setg("wbufT%d" % (c % 2), "pe")
                p.add("vector", lambda h: h.tensor_copy(stgr, psTr[:]),
                      waits=[("pe", p.c("pe")), p.g("stg")], inc=("vv", 1))
                p.setg("psTr", "vv")
                writer(c)
                p.setg("stg", "ws")

        def wr_io(dst_d):
            def wr(c):
                p.dma("sync", dst_d[c].rearrange("(kc q) e -> q kc e", q=128),
                      stgr3, waits=[("vv", p.c("vv"))])
            return wr

        def wr_wo(c):
            p.dma("sync", w_outT_d.rearrange("(kc q) e -> q kc e", q=128)
                  [:, :, c * 128:(c + 1) * 128], stgr3,
                  waits=[("vv", p.c("vv"))])

        w_stream_T(w_in_x, wr_io(w_in_xT_d))
        w_stream_T(w_in_z, wr_io(w_in_zT_d))
        w_stream_T(w_out, wr_wo)

        bufx = wtbuf[0:96, 0, :]
        w = p.dma("gpsimd", bufx, w_x, waits=[p.g("wbufT0")])
        pe_T(bufx, 8, psTr, idr, [w], "r")
        p.add("vector", lambda h: h.tensor_copy(
            w_xT[:],
            psTr[:].rearrange("p (a b) -> p a b", a=8)[:, :, 0:96]),
            waits=[("pe", p.c("pe"))], inc=("vv", 1))
        p.setg("psTr", "vv")
        bufd = wtbuf[:, 1, 0:512].rearrange("p (a b) -> p a b", a=NDC)
        w = p.dma("gpsimd", bufd,
                  w_dt.rearrange("(dc q) r -> q dc r", q=128),
                  waits=[p.g("wbufT1")])
        for dc in range(NDC):
            p.add("tensor", lambda h, dc=dc: h.transpose(
                psTr[0:64, dc * 128:(dc + 1) * 128], bufd[:, dc, :], idr[:]),
                waits=([w, p.g("psTr")] if dc == 0 else []),
                inc=(("pe", 1) if dc == NDC - 1 else None))
        p.add("vector", lambda h: h.tensor_copy(
            w_dtT[:].rearrange("p a b -> p (a b)"), psTr[0:64, :]),
            waits=[("pe", p.c("pe"))], inc=("vv", 1))
        p.setg("psTr", "vv")
        p.setg("kxarena", "pe")
        p.setg("wcol0", "pe")
        p.setg("wcol1", "pe")

        def ln_chunk(xch, xnto, ldw, apply_wait):
            nm, m2, v1, sd, rc = (colscr[:, i:i + 1] for i in range(5))
            q0 = sq0[:, 0:1]
            p.add("vector", lambda h: h.tensor_reduce(nm, xch, AX.X, ALU.add),
                  waits=[ldw], inc=("vv", 1))
            p.add("vector", lambda h: h.tensor_scalar(
                nm, nm, -1.0 / DM, None, ALU.mult),
                waits=[("vv", p.c("vv"))], inc=("vv", 1))
            p.add("scalar", lambda h: h.activation(
                lnscr[:], xch, ACT.Square, accum_out=q0),
                waits=[ldw, p.g("lnq")], inc=("ss", 1))
            p.add("vector", lambda h: h.tensor_tensor(m2, nm, nm, ALU.mult),
                  waits=[("ss", p.c("ss")), ("vv", p.c("vv"))], inc=("vv", 1))
            p.add("vector", lambda h: h.tensor_scalar(
                v1, q0, 1.0 / DM, 1e-5, ALU.mult, ALU.add), inc=("vv", 1))
            p.add("vector", lambda h: h.tensor_tensor(
                v1, v1, m2, ALU.subtract),
                waits=[("vv", p.c("vv"))], inc=("vv", 1))
            p.setg("lnq", "vv")
            p.add("scalar", lambda h: h.activation(sd, v1, ACT.Sqrt),
                  waits=[("vv", p.c("vv"))], inc=("ss", 1))
            p.add("vector", lambda h: h.reciprocal(rc, sd),
                  waits=[("ss", p.c("ss"))], inc=("vv", 1))
            p.add("vector", lambda h: h.tensor_scalar(
                xnto, xch, nm, rc, ALU.add, ALU.mult),
                waits=[apply_wait, ("vv", p.c("vv"))], inc=("vv", 1))

        for tc in range(NOC):
            xch = av(0, [128, DM])
            w = p.dma("sync", xch, x_own[tc * 128:(tc + 1) * 128, :],
                      waits=[p.g("kxarena")])
            xnch = av(1024, [128, DM])
            ln_chunk(xch, xnch, w, p.g("kxarena"))
            mx, sx = colscr[:, 6:7], colscr[:, 7:8]
            p.add("vector", lambda h: h.tensor_reduce(
                mx, xnch, AX.X, ALU.max, apply_absolute_value=True),
                inc=("vv", 1))
            p.add("vector", lambda h: h.tensor_scalar(
                mx, mx, 1e-5, None, ALU.max),
                waits=[("vv", p.c("vv"))], inc=("vv", 1))
            p.add("vector", lambda h: h.reciprocal(sx, mx),
                  waits=[("vv", p.c("vv"))], inc=("vv", 1))
            p.add("vector", lambda h: h.tensor_scalar(
                sx, sx, 127.0, None, ALU.mult),
                waits=[("vv", p.c("vv"))], inc=("vv", 1))
            p.add("vector", lambda h, t=tc: h.tensor_scalar(
                rsx[:, t:t + 1], mx, 1.0 / 127.0, None, ALU.mult),
                inc=("vv", 1))
            ki = av(2048, [128, DM], I32)
            kb = av(3072, [128, DM], BF16)
            p.add("vector", lambda h, d=ki, x=xnch: h.tensor_scalar(
                d, x, sx, None, ALU.mult),
                waits=[("vv", p.c("vv"))], inc=("vv", 1))
            p.add("vector", lambda h, d=kb, s2_=ki: h.tensor_copy(d, s2_),
                  inc=("vv", 1))
            pe_T(kb, 8, psTb, idb, [("vv", p.c("vv")), ("gs", ldg)], "b")
            stb = av(3584, [128, NDC, 128], BF16)
            p.add("vector", lambda h, d=stb: h.tensor_copy(
                d.rearrange("p a b -> p (a b)"), psTb[:]),
                waits=[("pe", p.c("pe")), p.g("stbC")], inc=("vv", 1))
            p.setg("psTb", "vv")
            p.dma("sync", kx_d.rearrange("(kc q) t -> q kc t", q=128)
                  [:, :, tc * 128:(tc + 1) * 128], stb,
                  waits=[("vv", p.c("vv"))])
            p.setg("stbC", "ws")
            p.setg("kxarena", "vv")

        for wi, (wsrc, rows, fw) in enumerate(
                ((w_gate, INNER, DM), (w_up, INNER, DM), (w_down, DM, INNER))):
            acc = wsum[:, wi:wi + 1]
            p.add("vector", lambda h, a=acc: h.memset(a, 0.0), inc=("vv", 1))
            ncol = 2 if fw > 2048 else 1
            cw = fw // ncol
            for c in range(rows // 128):
                for j in range(ncol):
                    pp = (c * ncol + j) % 2
                    wch = av(pp * 2048, [128, cw])
                    w = p.dma("sync", wch,
                              wsrc[c * 128:(c + 1) * 128,
                                   j * cw:(j + 1) * cw],
                              waits=[p.g("qbuf%d" % pp)], sem="cq")
                    ab = av(4096, [128, cw])
                    p.add("scalar", lambda h, x=wch, o=ab: h.activation(
                        o, x, ACT.Abs, accum_out=colscr[:, 8:9]),
                        waits=[w, p.g("qabs")], inc=("ss", 1))
                    p.setg("qbuf%d" % pp, "ss")
                    p.add("vector", lambda h, a=acc: h.tensor_tensor(
                        a, a, colscr[:, 8:9], ALU.add),
                        waits=[("ss", p.c("ss")), ("vv", p.c("vv"))],
                        inc=("vv", 1))
                    p.setg("qabs", "vv")
        onescol = colscr[:, 9:10]
        p.add("vector", lambda h: h.memset(onescol, 1.0), inc=("vv", 1))
        ps, (pss, psv), pi = ps_acquire()
        p.add("tensor", lambda h, ps=ps: h.matmul(
            ps[0:1, 0:4], onescol, wsum[:, 0:4], start=True, stop=True),
            waits=[("vv", p.c("vv")), (pss, psv)], inc=("pe", 1))
        p.add("vector", lambda h, ps=ps: h.tensor_copy(
            m4stg[0:1, 0:4], ps[0:1, 0:4]),
            waits=[("pe", p.c("pe"))], inc=("vv", 1))
        ps_release(pi, "vv")
        p.dma("sync", mwrow_d, m4stg[0:1, 0:4], waits=[("vv", p.c("vv"))],
              sem="oq")
        p.dma("gpsimd", mw[:, 0:4], mwrow_d.partition_broadcast(128),
              waits=[("oq", p.c("oq"))])
        ldmw = p.c("gs")
        for wi in range(3):
            p.add("vector", lambda h, wi=wi: h.tensor_scalar(
                mw[:, wi:wi + 1], mw[:, wi:wi + 1], 1.0 / (INNER * DM), 1e-5,
                ALU.mult, ALU.max),
                waits=[("gs", ldmw), ("vv", p.c("vv"))], inc=("vv", 1))
        swcol = colscr[:, 10:11]
        for wi, (wsrc, dstd, rows, fw) in enumerate(
                ((w_gate, gT_g_d, INNER, DM), (w_up, gT_u_d, INNER, DM),
                 (w_down, gdT_d, DM, INNER))):
            p.add("vector", lambda h, wi=wi: h.reciprocal(
                swcol, mw[:, wi:wi + 1]),
                waits=[("vv", p.c("vv"))], inc=("vv", 1))
            swv = ("vv", p.c("vv"))
            ncol = 2 if fw > 2048 else 1
            cw = fw // ncol
            for c in range(rows // 128):
                for j in range(ncol):
                    pp = (c * ncol + j) % 2
                    wch = av(pp * 2048, [128, cw])
                    w = p.dma("sync", wch,
                              wsrc[c * 128:(c + 1) * 128,
                                   j * cw:(j + 1) * cw],
                              waits=[p.g("qbuf%d" % pp)], sem="cq")
                    gi = wch.bitcast(I32)
                    gb = av(6144, [128, cw], BF16)
                    p.add("vector", lambda h, x=wch, d=gi: h.tensor_scalar(
                        d, x, swcol, None, ALU.mult),
                        waits=[w, swv, p.g("qgb")], inc=("vv", 1))
                    p.add("vector", lambda h, s=gi, d=gb: h.tensor_scalar(
                        d, s, -1.0, 1.0, ALU.max, ALU.min), inc=("vv", 1))
                    p.setg("qbuf%d" % pp, "vv")
                    for g in range(cw // 1024):
                        pe_T(gb[:, g * 1024:(g + 1) * 1024], 8, psTb, idb,
                             [("vv", p.c("vv")), ("gs", ldg)], "b")
                        stb = av(8192, [128, NDC, 128], BF16)
                        p.add("vector", lambda h, d=stb: h.tensor_copy(
                            d.rearrange("p a b -> p (a b)"), psTb[:]),
                            waits=[("pe", p.c("pe")), p.g("stbD")],
                            inc=("vv", 1))
                        p.setg("psTb", "vv")
                        if g == cw // 1024 - 1:
                            p.setg("qgb", "vv")
                        if fw == DM:
                            dst = dstd[c].rearrange("(kc q) e -> q kc e", q=128)
                        else:
                            base = (j * 2048 + g * 1024) // 128
                            dst = dstd[c].rearrange(
                                "(kc q) e -> q kc e", q=128)[:, base:base + 8]
                        p.dma("sync", dst, stb, waits=[("vv", p.c("vv"))])
                        p.setg("stbD", "ws")
        p.setg("frontx0", "vv")
        p.setg("frontx1", "vv")
        p.setg("qdone_ss", "ss")
        p.guard["prologue_done"] = ("ws", p.c("ws"))


        def ffn_es_setup(es):
            gcol = av(0, [128, NDC, 512], BF16)
            ucol = av(2048, [128, NDC, 512], BF16)
            for j in range(4):
                w1 = p.dma("sync", gcol[:, :, j * 128:(j + 1) * 128],
                           gT_g_d[es * 4 + j].rearrange(
                               "(kc q) e -> q kc e", q=128),
                           waits=([p.g("gcol"), p.g("prologue_done")]
                                  if j == 0 else []), sem="gq")
            for j in range(4):
                w2 = p.dma("sync", ucol[:, :, j * 128:(j + 1) * 128],
                           gT_u_d[es * 4 + j].rearrange(
                               "(kc q) e -> q kc e", q=128),
                           waits=([p.g("ucol")] if j == 0 else []), sem="gq")
            return gcol, ucol, w2, w2

        def ffn_tb(es, tb, gcol, ucol, w1, w2):
            kxb = kxt[:, tb % 2]
            w3 = p.dma("sync", kxb, kx_d.rearrange(
                "(kc q) t -> q kc t", q=128)
                [:, :, tb * 128:(tb + 1) * 128],
                waits=[p.g("kxt%d" % (tb % 2))], sem="kq%d" % (tb % 2))
            psg, (pgs, pgv), pgi = ps_acquire()
            psu, (pus, puv), pui = ps_acquire()
            for kc in range(NDC):
                p.add("tensor", lambda h, ps=psg, kc=kc, g=gcol, k=kxb:
                      h.matmul(ps[:, 0:512], k[:, kc, :], g[:, kc, :],
                               start=(kc == 0), stop=(kc == NDC - 1)),
                      waits=([w1, w3, (pgs, pgv)] if kc == 0 else []),
                      inc=(("pe", 1) if kc == NDC - 1 else None))
            for kc in range(NDC):
                p.add("tensor", lambda h, ps=psu, kc=kc, g=ucol, k=kxb:
                      h.matmul(ps[:, 0:512], k[:, kc, :], g[:, kc, :],
                               start=(kc == 0), stop=(kc == NDC - 1)),
                      waits=([w2, (pus, puv)] if kc == 0 else []),
                      inc=(("pe", 1) if kc == NDC - 1 else None))
            p.setg("kxt%d" % (tb % 2), "pe")
            p.add("vector", lambda h, tb=tb: h.tensor_tensor(
                colscr[:, 11:12], mw[:, 0:1], rsx[:, tb:tb + 1],
                ALU.mult), waits=[p.g("combo")], inc=("vv", 1))
            gt = av(4096, [128, 512])
            p.add("scalar", lambda h, ps=psg, d=gt: h.activation(
                d, ps[:, 0:512], ACT.Silu, scale=colscr[:, 11:12]),
                waits=[("pe", p.c("pe")), ("vv", p.c("vv")), p.g("gt")],
                inc=("ss", 1))
            p.setg("combo", "ss")
            ps_release(pgi, "ss")
            gu = av(4608, [128, 512])
            p.add("vector", lambda h, ps=psu, g=gt, d=gu: h.tensor_tensor(
                d, g, ps[:, 0:512], ALU.mult),
                waits=[("ss", p.c("ss")), p.g("gu")], inc=("vv", 1))
            p.setg("gt", "vv")
            ps_release(pui, "vv")
            p.add("vector", lambda h, d=gu: h.tensor_reduce(
                colscr[:, 12:13], d, AX.X, ALU.max,
                apply_absolute_value=True), inc=("vv", 1))
            p.add("vector", lambda h, tb=tb: h.tensor_tensor(
                mxgu[:, tb:tb + 1], mxgu[:, tb:tb + 1],
                colscr[:, 12:13], ALU.max),
                waits=[("vv", p.c("vv"))], inc=("vv", 1))
            p.dma("sync", gu_d[tb * 128:(tb + 1) * 128,
                               es * 512:(es + 1) * 512], gu,
                  waits=[("vv", p.c("vv"))], sem="eq")
            p.setg("gu", "eq")

        p.add("vector", lambda h: h.memset(mxgu[:], 0.0), inc=("vv", 1))
        for b in range(NB):
            t0 = b * TBL
            for tc in range(NTC):
                xch = av(8192 + (tc % 2) * 1024, [128, DM])
                w = p.dma("sync", xch,
                          x_full[t0 + tc * 128: t0 + (tc + 1) * 128, :],
                          waits=[p.g("frontx%d" % (tc % 2))],
                          sem="xq%d" % (tc % 2))
                ln_chunk(xch, xn_tm[:], w, p.g("xn_tm"))
                p.setg("frontx%d" % (tc % 2), "vv")
                if debug:
                    p.dma("sync", dr["dbg_xraw"][t0 + tc * 128:
                                                 t0 + (tc + 1) * 128, :],
                          xch, waits=[("vv", p.c("vv"))])
                    p.dma("sync", dr["dbg_xntm"][t0 + tc * 128:
                                                 t0 + (tc + 1) * 128, :],
                          xn_tm[:], waits=[("vv", p.c("vv"))])
                p.setg("frontx", "vv")
                pe_T(xn_tm[:], 8, psTr, idr,
                     [("vv", p.c("vv")), p.g("xnfm_rd"), ("gs", ldg)], "r")
                p.setg("xn_tm", "pe")
                p.add("vector", lambda h, t=tc: h.tensor_copy(
                    xn_fm[:, :, t * 128:(t + 1) * 128],
                    psTr[:].rearrange("p (a b) -> p a b", a=8)),
                    waits=[("pe", p.c("pe"))], inc=("vv", 1))
                p.setg("psTr", "vv")
            xn_ready = ("vv", p.c("vv"))
            if debug:
                p.dma("sync", dr["dbg_xn"].rearrange(
                    "(dc q) t -> q dc t", q=128)[:, :, t0:t0 + TBL],
                    xn_fm[:], waits=[xn_ready])

            for dc in range(NDC):
                wb = wcol[:, dc % 2]
                w = p.dma("sync", wb,
                          w_in_xT_d[dc].rearrange("(kc q) e -> q kc e", q=128),
                          waits=[p.g("wcol%d" % (dc % 2))],
                          sem="wq%d" % (dc % 2))
                ps, (pss, psv), pi = ps_acquire()
                for kc in range(NDC):
                    p.add("tensor", lambda h, ps=ps, kc=kc, wb=wb: h.matmul(
                        ps[:, 0:TBL], wb[:, kc, :], xn_fm[:, kc, :],
                        start=(kc == 0), stop=(kc == NDC - 1)),
                        waits=([w, xn_ready, (pss, psv)] if kc == 0 else []),
                        inc=(("pe", 1) if kc == NDC - 1 else None))
                p.setg("wcol%d" % (dc % 2), "pe")
                xcb = xc[:, dc % 2]
                p.add("vector", lambda h, ps=ps, x=xcb: h.tensor_scalar(
                    x[:, 3:3 + TBL], ps[:, 0:TBL], 0.0, 1.0,
                    ALU.max, ALU.min),
                    waits=[("pe", p.c("pe"))], inc=("vv", 1))
                ps_release(pi, "vv")
                p.add("vector", lambda h, dc=dc, x=xcb: h.tensor_copy(
                    x[:, 0:3], xtail[:, dc, :]), inc=("vv", 1))
                cb = cbuf[:, dc % 2]
                p.add("vector", lambda h, dc=dc, cb=cb, x=xcb: h.tensor_scalar(
                    cb, x[:, 0:TBL], convw[:, dc, 0:1], convb[:, dc, 0:1],
                    ALU.mult, ALU.add),
                    waits=[p.g("cb_m3"), ("ws", ldc), ("vv", p.c("vv"))],
                    inc=("vv", 1))
                for k in range(1, 4):
                    p.add("vector", lambda h, dc=dc, k=k, cb=cb, x=xcb:
                          h.scalar_tensor_tensor(
                              cb, x[:, k:k + TBL], convw[:, dc, k:k + 1],
                              cb, ALU.mult, ALU.add), inc=("vv", 1))
                p.add("vector", lambda h, dc=dc, x=xcb: h.tensor_copy(
                    xtail[:, dc, :], x[:, TBL:TBL + 3]), inc=("vv", 1))
                p.add("scalar", lambda h, dc=dc, cb=cb: h.activation(
                    cbuf[:, (dc + 1) % 2], cb, ACT.Silu),
                    waits=[("vv", p.c("vv")), p.g("cb2%d" % ((dc + 1) % 2))],
                    inc=("ss", 1))
                p.add("vector", lambda h, dc=dc: h.tensor_scalar(
                    u_sb[:, dc, :], cbuf[:, (dc + 1) % 2], 0.0, 1.0,
                    ALU.max, ALU.min),
                    waits=[("ss", p.c("ss")), p.g("u_wr")], inc=("vv", 1))
                p.setg("cb2%d" % ((dc + 1) % 2), "vv")
            u_ready = ("vv", p.c("vv"))
            if debug:
                p.dma("sync", dr["dbg_u"].rearrange(
                    "(dc q) t -> q dc t", q=128)[:, :, t0:t0 + TBL],
                    u_sb[:], waits=[u_ready])

            ps, (pss, psv), pi = ps_acquire()
            for kc in range(NDC):
                p.add("tensor", lambda h, ps=ps, kc=kc: h.matmul(
                    ps[0:96, 0:TBL], w_xT[:, kc, :], u_sb[:, kc, :],
                    start=(kc == 0), stop=(kc == NDC - 1)),
                    waits=([u_ready, (pss, psv)] if kc == 0 else []),
                    inc=(("pe", 1) if kc == NDC - 1 else None))
            p.add("vector", lambda h, ps=ps: h.tensor_copy(
                xdstg[:], ps[0:96, 0:TBL]),
                waits=[("pe", p.c("pe")), p.g("xdstg")], inc=("vv", 1))
            ps_release(pi, "vv")
            p.dma("sync", ar_in_d[b], xdstg[:], waits=[("vv", p.c("vv"))])
            p.setg("xdstg", "ws")
            p.add("gpsimd", lambda h, b=b: h.collective_compute(
                "AllReduce", ALU.add, replica_groups=RG,
                ins=[ar_in_d[b]], outs=[ar_out_d[b]]),
                waits=[("ws", p.c("ws"))], inc=("cc", 1))
            cc_b = ("cc", p.c("cc"))
            w = p.dma("sync", xdbl[:], ar_out_d[b],
                      waits=[cc_b, p.g("xdbl")], sem="bq")
            if debug:
                p.dma("sync", dr["dbg_xdbl"][b], xdbl[:], waits=[w])
            p.add("vector", lambda h: h.tensor_scalar(
                dtraw[:], xdbl[0:64, :], 0.0, 1.0, ALU.max, ALU.min),
                waits=[w, p.g("dtraw_rd")], inc=("vv", 1))
            dtraw_ready = ("vv", p.c("vv"))
            p.setg("xdbl", "vv")

            for dc in range(NDC):
                ps, (pss, psv), pi = ps_acquire()
                p.add("tensor", lambda h, ps=ps, dc=dc: h.matmul(
                    ps[:, 0:TBL], w_dtT[:, dc, :], dtraw[:],
                    start=True, stop=True),
                    waits=[dtraw_ready, (pss, psv)], inc=("pe", 1))
                if dc == NDC - 1:
                    p.setg("dtraw_rd", "pe")
                p.add("scalar", lambda h, ps=ps, dc=dc: h.activation(
                    cbuf[:, 0], ps[:, 0:TBL], ACT.Exp, bias=bdt[:, dc, :]),
                    waits=[("pe", p.c("pe")), p.g("cb0"), ("ws", ldc)],
                    inc=("ss", 1))
                ps_release(pi, "ss")
                p.add("scalar", lambda h: h.activation(
                    cbuf[:, 1], cbuf[:, 0], ACT.Ln, bias=1.0), inc=("ss", 1))
                p.setg("cb0", "ss")
                p.add("vector", lambda h, dc=dc: h.tensor_scalar(
                    dtc[:, dc, :], cbuf[:, 1], 1e-4, 20.0, ALU.max, ALU.min),
                    waits=[("ss", p.c("ss"))], inc=("vv", 1))
                p.add("vector", lambda h, dc=dc: h.tensor_tensor(
                    dtu[:, dc, :], dtc[:, dc, :], u_sb[:, dc, :], ALU.mult),
                    inc=("vv", 1))
            dt_ready = ("vv", p.c("vv"))
            p.setg("cb_m3", "ss")
            if debug:
                p.dma("sync", dr["dbg_dt"].rearrange(
                    "(dc q) t -> q dc t", q=128)[:, :, t0:t0 + TBL], dtc[:],
                    waits=[dt_ready])

            gcol_b, ucol_b, wg1, wg2 = ffn_es_setup(b)
            yacc = yacc_sb[:]
            p.add("vector", lambda h, y=yacc: h.memset(
                y.rearrange("p a b -> p (a b)"), 0.0), inc=("vv", 1))
            ffn_tb(b, 0, gcol_b, ucol_b, wg1, wg2)
            ffn_tb(b, 1, gcol_b, ucol_b, wg1, wg2)
            for n in range(NST):
                nb_ = n % 2
                w1 = p.dma("gpsimd", brep[:, nb_],
                           ar_out_d[b, DTR + n:DTR + n + 1, :]
                           .partition_broadcast(128),
                           waits=[cc_b, p.g("brep%d" % nb_)])
                w2 = p.dma("gpsimd", crep[:, nb_],
                           ar_out_d[b, DTR + NST + n:DTR + NST + n + 1, :]
                           .partition_broadcast(128),
                           waits=[p.g("crep%d" % nb_)])
                for dc in range(NDC):
                    i = dc % 2
                    p.add("scalar", lambda h, dc=dc, n=n, i=i: h.activation(
                        dA_sb[:, i], dtc[:, dc, :], ACT.Exp,
                        scale=negA[:, dc, n:n + 1]),
                        waits=[dt_ready, p.g("dA%d" % i)], inc=("ss", 1))
                    sx_ = ("ss", p.c("ss"))
                    p.add("vector", lambda h, dc=dc, i=i, nb_=nb_:
                          h.tensor_tensor(dBu_sb[:, i], dtu[:, dc, :],
                                          brep[:, nb_], ALU.mult),
                          waits=[w2, sx_], inc=("vv", 1))
                    p.add("vector", lambda h, dc=dc, n=n, i=i:
                          h.tensor_tensor_scan(
                              hseq_sb[:, i], dA_sb[:, i], dBu_sb[:, i],
                              hst[:, dc, n:n + 1], ALU.mult, ALU.add),
                          waits=[p.g("hstc%d" % i)], inc=("vv", 1))
                    p.setg("dA%d" % i, "vv")
                    p.add("vector", lambda h, dc=dc, i=i, nb_=nb_:
                          h.tensor_tensor(cstage[:, dc, :], hseq_sb[:, i],
                                          crep[:, nb_], ALU.mult),
                          waits=[w2], inc=("vv", 1))
                    p.add("vector", lambda h, dc=dc, y=yacc:
                          h.tensor_tensor(y[:, dc, :], y[:, dc, :],
                                          cstage[:, dc, :], ALU.add),
                          inc=("vv", 1))
                    p.add("vector", lambda h, dc=dc, n=n, i=i: h.tensor_copy(
                        hst[:, dc, n:n + 1], hseq_sb[:, i, TBL - 1:TBL]),
                        inc=("vv", 1))
                    p.setg("hstc%d" % i, "vv")
                p.setg("brep%d" % nb_, "vv")
                p.setg("crep%d" % nb_, "vv")
                if n < NST - 2:
                    ffn_tb(b, n + 2, gcol_b, ucol_b, wg1, wg2)
            p.setg("gcol", "pe")
            p.setg("ucol", "pe")
            scan_done = ("vv", p.c("vv"))
            if debug:
                p.dma("sync", dr["dbg_ys"].rearrange(
                    "(dc q) t -> q dc t", q=128)[:, :, t0:t0 + TBL],
                    yacc, waits=[scan_done])

            for dc in range(NDC):
                wb = wcol[:, dc % 2]
                w = p.dma("sync", wb,
                          w_in_zT_d[dc].rearrange("(kc q) e -> q kc e", q=128),
                          waits=[p.g("wcol%d" % (dc % 2))],
                          sem="wq%d" % (dc % 2))
                ps, (pss, psv), pi = ps_acquire()
                for kc in range(NDC):
                    p.add("tensor", lambda h, ps=ps, kc=kc, wb=wb: h.matmul(
                        ps[:, 0:TBL], wb[:, kc, :], xn_fm[:, kc, :],
                        start=(kc == 0), stop=(kc == NDC - 1)),
                        waits=([w, (pss, psv)] if kc == 0 else []),
                        inc=(("pe", 1) if kc == NDC - 1 else None))
                p.setg("wcol%d" % (dc % 2), "pe")
                p.add("vector", lambda h, ps=ps: h.tensor_scalar(
                    zscr[:, 0], ps[:, 0:TBL], 0.0, 1.0, ALU.max, ALU.min),
                    waits=[("pe", p.c("pe")), p.g("z0")], inc=("vv", 1))
                ps_release(pi, "vv")
                p.add("scalar", lambda h: h.activation(
                    zscr[:, 1], zscr[:, 0], ACT.Silu),
                    waits=[("vv", p.c("vv")), p.g("z1")], inc=("ss", 1))
                sgate = ("ss", p.c("ss"))
                p.add("vector", lambda h, dc=dc, y=yacc: h.scalar_tensor_tensor(
                    zscr[:, 0], u_sb[:, dc, :], dvec[:, dc, :], y[:, dc, :],
                    ALU.mult, ALU.add),
                    waits=[scan_done, ("ws", ldc), sgate],
                    inc=("vv", 1))
                p.add("vector", lambda h: h.tensor_scalar(
                    zscr[:, 0], zscr[:, 0], 0.0, 1.0, ALU.max, ALU.min),
                    inc=("vv", 1))
                p.add("vector", lambda h: h.tensor_tensor(
                    zscr[:, 0], zscr[:, 0], zscr[:, 1], ALU.mult),
                    waits=[sgate], inc=("vv", 1))
                p.setg("z1", "vv")
                p.add("vector", lambda h, dc=dc: h.tensor_scalar(
                    u_sb[:, dc, :], zscr[:, 0], 0.0, 1.0, ALU.max, ALU.min),
                    inc=("vv", 1))
                p.setg("z0", "vv")
            y_ready = ("vv", p.c("vv"))
            p.setg("yacc_arena", "vv")
            if debug:
                p.dma("sync", dr["dbg_y"].rearrange(
                    "(dc q) t -> q dc t", q=128)[:, :, t0:t0 + TBL],
                    u_sb[:], waits=[y_ready])

            woutb = xn_fm[:].rearrange("p a b -> p (a b)")
            wv = woutb.rearrange("p (a b) -> p a b", a=NDC)
            for half in range(2):
                w = p.dma("sync", wv,
                          w_outT_d.rearrange("(kc q) e -> q kc e", q=128)
                          [:, :, half * 512:(half + 1) * 512],
                          waits=([y_ready, p.g("woutb")] if half == 0
                                 else [p.g("woutb")]), sem="vq")
                for ts in range(NTC):
                    ps, (pss, psv), pi = ps_acquire()
                    for kc in range(NDC):
                        p.add("tensor", lambda h, ps=ps, ts=ts, kc=kc:
                              h.matmul(
                                  ps[:, 0:512],
                                  u_sb[:, kc, ts * 128:(ts + 1) * 128],
                                  wv[:, kc, :],
                                  start=(kc == 0), stop=(kc == NDC - 1)),
                              waits=([w, y_ready, (pss, psv)]
                                     if kc == 0 else []),
                              inc=(("pe", 1) if kc == NDC - 1 else None))
                    p.add("scalar", lambda h, ps=ps, half=half:
                          h.activation(m4stg[:, half * 512:(half + 1) * 512],
                                       ps[:, 0:512], ACT.Copy),
                          waits=[("pe", p.c("pe")), p.g("m4s%d" % half)],
                          inc=("ss", 1))
                    ps_release(pi, "ss")
                    p.dma("sync",
                          m4p_d[t0 + ts * 128: t0 + (ts + 1) * 128,
                                half * 512:(half + 1) * 512],
                          m4stg[:, half * 512:(half + 1) * 512],
                          waits=[("ss", p.c("ss"))], sem="mq")
                    if debug:
                        p.dma("sync",
                              dr["dbg_m4p"][t0 + ts * 128: t0 + (ts + 1) * 128,
                                            half * 512:(half + 1) * 512],
                              m4stg[:, half * 512:(half + 1) * 512])
                    p.setg("m4s%d" % half, "mq")
                p.setg("woutb", "pe")
            p.setg("xnfm_rd", "pe")
            p.setg("u_wr", "pe")

        p.add("gpsimd", lambda h: h.collective_compute(
            "ReduceScatter", ALU.add, replica_groups=RG,
            ins=[m4p_d], outs=[rs_out_d]),
            waits=[("ws", p.c("ws")), ("mq", p.c("mq"))], inc=("cc", 1))
        cc_rs = ("cc", p.c("cc"))

        if debug:
            for ch in range(8):
                wmp = p.dma("sync", m4stg[:],
                            m4p_d[ch * 512:ch * 512 + 128, :],
                            waits=[cc_rs, ("ss", p.c("ss"))])
                p.dma("sync", dr["dbg_m4post"][ch], m4stg[:], waits=[wmp])
        blocks_ws = ("ws", p.c("ws"))
        blocks_pe = ("pe", p.c("pe"))
        for tb in range(NTB):
            guin = av((tb % 2) * 4096, [128, INNER])
            w = p.dma("sync", guin, gu_d[tb * 128:(tb + 1) * 128, :],
                      waits=[p.g("guin%d" % (tb % 2)), blocks_pe, p.g("gu")]
                      if tb <= 1 else [p.g("guin%d" % (tb % 2))], sem="uq")
            ac, ia, th, mxc, s2, bt = (colscr[:, i:i + 1]
                                       for i in range(13, 19))
            p.add("vector", lambda h, tb=tb: h.tensor_tensor(
                ac, mw[:, 1:2], rsx[:, tb:tb + 1], ALU.mult), inc=("vv", 1))
            p.add("vector", lambda h: h.reciprocal(ia, ac),
                  waits=[("vv", p.c("vv"))], inc=("vv", 1))
            p.add("vector", lambda h: h.tensor_scalar(
                th, ia, 1e-5, None, ALU.mult),
                waits=[("vv", p.c("vv"))], inc=("vv", 1))
            p.add("vector", lambda h, tb=tb: h.tensor_tensor(
                mxc, mxgu[:, tb:tb + 1], th, ALU.max),
                waits=[("vv", p.c("vv"))], inc=("vv", 1))
            p.add("vector", lambda h: h.reciprocal(s2, mxc),
                  waits=[("vv", p.c("vv"))], inc=("vv", 1))
            p.add("vector", lambda h: h.tensor_scalar(
                s2, s2, 127.0, None, ALU.mult),
                waits=[("vv", p.c("vv"))], inc=("vv", 1))
            p.add("vector", lambda h: h.tensor_tensor(
                bt, mw[:, 2:3], ac, ALU.mult), inc=("vv", 1))
            p.add("vector", lambda h: h.tensor_tensor(
                bt, bt, mxc, ALU.mult),
                waits=[("vv", p.c("vv"))], inc=("vv", 1))
            p.add("vector", lambda h: h.tensor_scalar(
                bt, bt, 1.0 / 127.0, None, ALU.mult),
                waits=[("vv", p.c("vv"))], inc=("vv", 1))
            p.dma("sync",
                  beta_d[:, tb * 128:(tb + 1) * 128].rearrange("a b -> b a"),
                  bt, waits=[("vv", p.c("vv"))], sem="eq")
            ki = guin.bitcast(I32)
            kb = av(8192, [128, INNER], BF16)
            p.add("vector", lambda h, s=guin, d=ki: h.tensor_scalar(
                d, s, s2, None, ALU.mult),
                waits=([w, ("vv", p.c("vv")), blocks_ws] if tb == 0
                       else [w, ("vv", p.c("vv"))]), inc=("vv", 1))
            p.add("vector", lambda h, s=ki, d=kb: h.tensor_copy(d, s),
                  waits=[p.g("kbq")], inc=("vv", 1))
            p.setg("guin%d" % (tb % 2), "vv")
            for g in range(4):
                pe_T(kb[:, g * 1024:(g + 1) * 1024], 8, psTb, idb,
                     [("vv", p.c("vv")), ("gs", ldg)], "b")
                stb = av(6656, [128, NDC, 128], BF16)
                p.add("vector", lambda h, d=stb: h.tensor_copy(
                    d.rearrange("p a b -> p (a b)"), psTb[:]),
                    waits=[("pe", p.c("pe")), p.g("stbq")], inc=("vv", 1))
                p.setg("psTb", "vv")
                if g == 3:
                    p.setg("kbq", "vv")
                p.dma("sync", kgu_d.rearrange(
                    "(g kc q) t -> q g kc t", q=128, kc=8)[:, g, :,
                    tb * 128:(tb + 1) * 128], stb,
                    waits=[("vv", p.c("vv"))], sem="sq")
                p.setg("stbq", "sq")
                p.setg("kgu_st", "sq")

        beta_ws = ("eq", p.c("eq"))
        ffn_ws = ("ws", p.c("ws"))

        def final_tc(tc):
            a1 = zscr[:].rearrange("p a b -> p (a b)")
            a2 = hseq_sb[:].rearrange("p a b -> p (a b)")
            a3 = dBu_sb[:].rearrange("p a b -> p (a b)")
            wa = p.dma("scalar", a1, rs_out_d[tc * 128:(tc + 1) * 128, :],
                       waits=[cc_rs, p.g("fin"), p.g("fa")], sem="fq")
            p.dma("scalar", a2, ffn_d[tc * 128:(tc + 1) * 128, :],
                  waits=[ffn_ws] if tc % 4 == 0 else [], sem="fq")
            p.dma("scalar", a3, x_own[tc * 128:(tc + 1) * 128, :], sem="fq")
            ldf = ("fq", p.c("fq"))
            p.add("vector", lambda h, a=a1, c2=a2: h.tensor_tensor(
                a, a, c2, ALU.add), waits=[ldf], inc=("vv", 1))
            p.add("vector", lambda h, a=a1, c3=a3: h.tensor_tensor(
                a, a, c3, ALU.add), inc=("vv", 1))
            p.setg("fa", "vv")
            p.dma("scalar", out[tc * 128:(tc + 1) * 128, :], a1,
                  waits=[("vv", p.c("vv"))], sem="oq")
            p.setg("fin", "oq")

        for ts in range(NTS):
            kts0 = av(0, [128, 16, 512], BF16)
            kts1 = av(4096, [128, 16, 512], BF16)
            wk0 = p.dma("sync", kts0, kgu_d.rearrange(
                "(es q) t -> q es t", q=128)[:, 0:16,
                ts * 512:(ts + 1) * 512],
                waits=[p.g("ktsh0"), p.g("guin0"), p.g("guin1"),
                       p.g("kgu_st")] if ts == 0 else [p.g("ktsh0")],
                sem="tq")
            wk1 = p.dma("sync", kts1, kgu_d.rearrange(
                "(es q) t -> q es t", q=128)[:, 16:32,
                ts * 512:(ts + 1) * 512],
                waits=[p.g("ktsh1")], sem="tq")
            brp = brep[:, 0]
            w2 = p.dma("gpsimd", brp,
                       beta_d[:, ts * 512:(ts + 1) * 512]
                       .partition_broadcast(128),
                       waits=[beta_ws, p.g("brep0"), p.g("brp")])
            for ec in range(NDC):
                gdc = av(8192, [128, 32, 128], BF16)
                w3 = p.dma("sync", gdc, gdT_d[ec].rearrange(
                    "(es q) e -> q es e", q=128),
                    waits=([p.g("gdc"), ("vv", p.c("vv")), ("ss", p.c("ss"))]
                           if (ts == 0 and ec == 0) else [p.g("gdc")]),
                    sem="dq")
                ps, (pss, psv), pi = ps_acquire()
                for es in range(32):
                    kbuf = kts0 if es < 16 else kts1
                    p.add("tensor", lambda h, ps=ps, es=es, g=gdc, k=kbuf:
                          h.matmul(ps[:, 0:512], g[:, es, :],
                                   k[:, es % 16, :],
                                   start=(es == 0), stop=(es == 31)),
                          waits=([wk0, w3, (pss, psv)] if es == 0 else
                                 ([wk1] if es == 16 else [])),
                          inc=(("pe", 1) if es == 31 else None))
                p.setg("gdc", "pe")
                if ec == NDC - 1:
                    p.setg("ktsh0", "pe")
                    p.setg("ktsh1", "pe")
                fsc = crep[:, 0]
                p.add("vector", lambda h, ps=ps, d=fsc: h.tensor_tensor(
                    d, ps[:, 0:512], brp, ALU.mult),
                    waits=[("pe", p.c("pe")), w2, p.g("fsc")], inc=("vv", 1))
                ps_release(pi, "vv")
                ps2_, (p2s, p2v), pi2 = ps_acquire()
                for sx2 in range(4):
                    p.add("tensor", lambda h, s=sx2, src=fsc, ps=ps2_:
                          h.transpose(ps[:, s * 128:(s + 1) * 128],
                                      src[:, s * 128:(s + 1) * 128], idf[:]),
                          waits=([("vv", p.c("vv")), (p2s, p2v), ("ws", ldc)]
                                 if sx2 == 0 else []),
                          inc=(("pe", 1) if sx2 == 3 else None))
                p.setg("fsc", "pe")
                ftm = dA_sb[:, 0].rearrange("p (a b) -> p a b", a=4)
                p.add("vector", lambda h, d=ftm, ps=ps2_: h.tensor_copy(
                    d.rearrange("p a b -> p (a b)"), ps[:, 0:512]),
                    waits=[("pe", p.c("pe")), p.g("ftm")], inc=("vv", 1))
                ps_release(pi2, "vv")
                p.dma("sync", ffn_d.rearrange(
                    "(sq q) e -> q sq e", q=128)[:, ts * 4:ts * 4 + 4,
                    ec * 128:(ec + 1) * 128], ftm,
                    waits=[("vv", p.c("vv"))])
                p.setg("ftm", "ws")
            p.setg("brp", "vv")
            ffn_ws = ("ws", p.c("ws"))
            if ts > 0:
                for tc in range((ts - 1) * 4, ts * 4):
                    final_tc(tc)
        for tc in range((NTS - 1) * 4, NOC):
            final_tc(tc)
        fin = p.c("oq")
        p.add("sync", lambda h: None, waits=[("oq", fin)])
        p.add("scalar", lambda h: None, waits=[("oq", fin)])

        names = set(p.cnt)
        for ops in p.ops.values():
            for _, waits, inc in ops:
                for nm_, _v in waits:
                    names.add(nm_)
                if inc:
                    names.add(inc[0])
        sems = {nm_: sem(nm_) for nm_ in sorted(names)}
        p.emit(block, sems)
    return nc, dr


def make_in_maps(inputs):
    eye = np.eye(128, dtype=np.float32)
    maps = []
    for c in range(8):
        b, m = c >> 1, c & 1
        dh = slice(m * DH, (m + 1) * DH)
        maps.append({
            "x_full": np.ascontiguousarray(inputs["L0_attn_hidden_states"][b]),
            "x_own": np.ascontiguousarray(
                inputs["L0_attn_hidden_states"][b][m * TOWN:(m + 1) * TOWN]),
            "w_in_x": np.ascontiguousarray(inputs["L0_attn_W_in"][dh]),
            "w_in_z": np.ascontiguousarray(
                inputs["L0_attn_W_in"][2048 + m * DH: 2048 + (m + 1) * DH]),
            "conv_w": np.ascontiguousarray(inputs["L0_attn_conv_w"][dh, 0, :]),
            "conv_b": np.ascontiguousarray(
                inputs["L0_attn_conv_b"][dh].reshape(DH, 1)),
            "w_x": np.ascontiguousarray(inputs["L0_attn_W_x"][:, dh]),
            "w_dt": np.ascontiguousarray(inputs["L0_attn_W_dt"][dh]),
            "b_dt": np.ascontiguousarray(
                inputs["L0_attn_b_dt"][dh].reshape(DH, 1)),
            "w_out": np.ascontiguousarray(inputs["L0_attn_W_out"][:, dh]),
            "a_log": np.ascontiguousarray(inputs["L0_attn_A_log"][dh]),
            "d_vec": np.ascontiguousarray(
                inputs["L0_attn_D"][dh].reshape(DH, 1)),
            "w_gate": inputs["L0_ffn_w_gate"],
            "w_up": inputs["L0_ffn_w_up"],
            "w_down": inputs["L0_ffn_w_down"],
            "ident": eye,
        })
    return maps


def kernel(**inputs):
    nc, _ = build_program()
    maps = make_in_maps(inputs)
    res = run_bass_kernel_spmd(nc, maps, core_ids=list(range(8)))
    outa = np.empty((B, T, DM), dtype=np.float32)
    for c in range(8):
        b, m = c >> 1, c & 1
        outa[b, m * TOWN:(m + 1) * TOWN] = res.results[c]["out"]
    return outa
